# version 05
# baseline (speedup 1.0000x reference)
import math
import os as _os
import sys as _sys
import contextlib
import numpy as np
_os.environ.setdefault("GAUGE_NTFF_IGNORE_DMA", "1")
import concourse.bass as bass
import concourse.mybir as mybir
from concourse.bass_utils import run_bass_kernel_spmd

F32 = mybir.dt.float32
BF16 = mybir.dt.bfloat16
AF = mybir.ActivationFunctionType
ALU = mybir.AluOpType
AX = mybir.AxisListType.X

D = 1024
FB = 8
NCORE = 8
LN_EPS = 1e-5


def _names(aps):
    out = []
    for a in aps:
        if a is None:
            continue
        if isinstance(a, str):
            out.append(a)
            continue
        t = getattr(a, "tensor", None)
        if t is not None:
            out.append(t.name)
    return out


class KB:
    ENGINES = ("tensor", "scalar", "vector", "gpsimd", "sync")
    QUEUES = ("q_gpsimd", "q_sync", "q_scalar")

    def __init__(self, nc, T):
        self.nc = nc
        self.T = T
        self.streams = {e: [] for e in self.ENGINES}
        self.lastw = {e: {} for e in self.ENGINES}
        self.es = contextlib.ExitStack()
        self.sbufs = {}
        self.psums = {}
        self._sems = {}
        self.ssem = {}
        for e in self.ENGINES:
            self.ssem[e] = self._sem(f"S_{e}")
        for q in self.QUEUES:
            self.ssem[q] = self._sem(f"S_{q}")
        self.scount = {k: 0 for k in self.ssem}
        self.last_writer = {}
        self.readers = {}
        self.scope = None

    def sbuf(self, name, shape, dtype, global_scope=False):
        if name in self.sbufs:
            t, sh, dt = self.sbufs[name]
            assert sh == list(shape) and dt == dtype, f"sbuf {name} clash"
            return t
        if self.scope is None or global_scope:
            t = self.es.enter_context(self.nc.sbuf_tensor(name, list(shape), dtype))
        else:
            t = self.scope.sbuf_tensor(name, list(shape), dtype)
        self.sbufs[name] = (t, list(shape), dtype)
        return t

    def barrier_all(self):
        toks = {st: self.scount[st] for st in self.ssem if self.scount[st] > 0}
        for eng in self.ENGINES:
            self.barrier_wait(eng, dict(toks))

    def begin_layer(self):
        assert self.scope is None
        self.scope = self.nc.auto_arena()

    def end_layer(self):
        self.barrier_all()
        self.scope.allocate()
        self.scope.free()
        self.scope = None

    def psum(self, name, shape, dtype=F32):
        if name in self.psums:
            return self.psums[name][0]
        t = self.es.enter_context(self.nc.psum_tensor(name, list(shape), dtype))
        self.psums[name] = (t, list(shape), dtype)
        return t

    def _sem(self, name):
        s = self.es.enter_context(self.nc.semaphore(name))
        self._sems[name] = s
        return s

    def _merge(self, waits, st, v):
        if v > waits.get(st, 0):
            waits[st] = v

    def _deps(self, stream, rnames, wnames):
        waits = {}
        for n in rnames:
            for st, v in self.last_writer.get(n, {}).items():
                if st == stream:
                    if st in self.ENGINES and st != "tensor":
                        self._merge(waits, st, v)
                    elif st not in self.ENGINES:
                        self._merge(waits, st, v)
                else:
                    self._merge(waits, st, v)
        for n in wnames:
            for st, v in self.last_writer.get(n, {}).items():
                if st != stream or st != "tensor":
                    self._merge(waits, st, v)
            for st, v in self.readers.get(n, {}).items():
                if st != stream or st != "tensor":
                    self._merge(waits, st, v)
        return waits

    def _commit(self, stream, val, rnames, wnames):
        for n in rnames:
            self.readers.setdefault(n, {})[stream] = val
        for n in wnames:
            self.last_writer[n] = {stream: val}
            self.readers[n] = {}

    def merge_writer(self, key, toks):
        d = dict(self.last_writer.get(key, {}))
        for st, v in toks.items():
            if v > d.get(st, 0):
                d[st] = v
        self.last_writer[key] = d
        self.readers[key] = {}

    def cur(self, *keys):
        out = {}
        for k in keys:
            for st, v in self.last_writer.get(k, {}).items():
                if v > out.get(st, 0):
                    out[st] = v
        return out

    def op(self, stream, eng, fn, reads=(), writes=(), inck=1):
        rnames = _names(reads)
        wnames = _names(writes)
        waits = self._deps(stream, rnames, wnames)
        if stream not in self.ENGINES and self.scount[stream] > 0:
            self._merge(waits, stream, self.scount[stream])
        self.scount[stream] += inck
        val = self.scount[stream]
        ih = self.ssem[stream]
        lw = self.lastw[eng]
        real = []
        for st in sorted(waits):
            v = waits[st]
            if lw.get(st, 0) >= v:
                continue
            lw[st] = v
            real.append((self._sems[f"S_{st}"], v))

        fr = _sys._getframe(1)
        while fr and fr.f_code.co_filename.endswith("kernel.py") and \
                fr.f_code.co_name in ("mm", "tr", "act", "tt", "ts", "stt", "copy",
                                      "red", "recip", "memset", "dma", "constcol",
                                      "op", "f"):
            fr = fr.f_back
        site = f"{fr.f_code.co_filename.rsplit('/',1)[-1]}:{fr.f_lineno}" if fr else "?"

        def go(e, real=real, fn=fn, ih=ih, inck=inck, site=site):
            for h, v in real:
                e.wait_ge(h, v)
            try:
                fn(e).then_inc(ih, inck)
            except Exception as ex:
                raise RuntimeError(f"op emitted at {site}: {ex}") from ex
        self.streams[eng].append(go)
        self._commit(stream, val, rnames, wnames)
        return (stream, val)

    def barrier_wait(self, eng, toks):
        lw = self.lastw[eng]
        real = []
        for st, v in toks.items():
            if lw.get(st, 0) >= v:
                continue
            lw[st] = v
            real.append((self._sems[f"S_{st}"], v))
        if real:
            def go(e, real=real):
                for h, v in real:
                    e.wait_ge(h, v)
            self.streams[eng].append(go)

    def mm(self, out, lhsT, rhs, start, stop, rk=(), wk=()):
        r = [lhsT, rhs] + ([] if start else [out]) + list(rk)
        return self.op("tensor", "tensor",
                       lambda e: e.matmul(out, lhsT, rhs, start=start, stop=stop),
                       reads=r, writes=[out] + list(wk))

    def tr(self, out, in_, ident):
        return self.op("tensor", "tensor",
                       lambda e: e.transpose(out, in_, ident),
                       reads=[in_, ident], writes=[out])

    def act(self, out, in_, func, bias=0.0, scale=1.0, accum=None, rk=(), wk=()):
        r = [in_] + ([bias] if not isinstance(bias, float) else []) + list(rk)
        w = [out] + ([accum] if accum is not None else []) + list(wk)

        def f(e):
            kw = {}
            if accum is not None:
                kw["accum_out"] = accum
            return e.activation(out, in_, func, bias=bias, scale=scale, **kw)
        return self.op("scalar", "scalar", f, reads=r, writes=w)

    def tt(self, out, a, b, alu, eng="vector", rk=(), wk=()):
        return self.op(eng, eng, lambda e: e.tensor_tensor(out, a, b, alu),
                       reads=[a, b] + list(rk), writes=[out] + list(wk))

    def ts(self, out, a, s1, op0, s2=None, op1=None, eng="vector", rk=(), wk=()):
        r = [a] + ([s1] if not isinstance(s1, float) else []) \
            + ([s2] if s2 is not None and not isinstance(s2, float) else []) + list(rk)

        def f(e):
            kw = {}
            if op1 is not None:
                kw["op1"] = op1
            return e.tensor_scalar(out, a, s1, s2, op0=op0, **kw)
        return self.op(eng, eng, f, reads=r, writes=[out] + list(wk))

    def stt(self, out, a, scalar, b, op0, op1):
        r = [a, b] + ([scalar] if not isinstance(scalar, float) else [])
        return self.op("vector", "vector",
                       lambda e: e.scalar_tensor_tensor(out, a, scalar, b, op0=op0, op1=op1),
                       reads=r, writes=[out])

    def copy(self, out, in_, eng="vector", rk=(), wk=()):
        if eng == "scalar":
            return self.op(eng, eng, lambda e: e.copy(out, in_),
                           reads=[in_] + list(rk), writes=[out] + list(wk))
        return self.op(eng, eng, lambda e: e.tensor_copy(out, in_),
                       reads=[in_] + list(rk), writes=[out] + list(wk))

    def red(self, out, in_, alu):
        return self.op("vector", "vector",
                       lambda e: e.tensor_reduce(out, in_, axis=AX, op=alu),
                       reads=[in_], writes=[out])

    def recip(self, out, in_):
        return self.op("vector", "vector", lambda e: e.reciprocal(out, in_),
                       reads=[in_], writes=[out])

    def constcol(self, value, n=128, dtype=F32):
        key = f"cst_{value!r}_{n}_{dtype}"
        if key in self.sbufs:
            return self.sbufs[key][0][:, 0:1]
        t = self.sbuf(key, [n, 1], dtype, global_scope=True)
        self.memset(t[:], float(value))
        return t[:, 0:1]

    def memset(self, ap, val, eng="vector"):
        return self.op(eng, eng, lambda e: e.memset(ap, val), reads=[], writes=[ap])

    def dma(self, q, out, in_, rk=(), wk=()):
        return self.op(f"q_{q}", q, lambda e: e.dma_start(out, in_),
                       reads=[in_] + list(rk), writes=[out] + list(wk), inck=16)

    def finalize(self, final_keys):
        self.barrier_wait("sync", self.cur(*final_keys))
        nc = self.nc
        blk = self.es.enter_context(nc.Block())
        streams = self.streams

        @blk.sync
        def _(e):
            for f in streams["sync"]:
                f(e)

        @blk.scalar
        def _(e):
            for f in streams["scalar"]:
                f(e)

        @blk.vector
        def _(e):
            for f in streams["vector"]:
                f(e)

        @blk.tensor
        def _(e):
            for f in streams["tensor"]:
                f(e)

        @blk.gpsimd
        def _(e):
            for f in streams["gpsimd"]:
                f(e)
        self.es.close()
        return nc


def make_consts(T):
    c = {}
    c["IDENT"] = np.eye(128, dtype=np.float32)
    c["ONES"] = np.ones((128, 128), dtype=np.float32)
    i = np.arange(128)[:, None]
    m = np.arange(1024)[None, :]
    c["AM01"] = (i <= (m - 512)).astype(np.float32)
    c["BAND01"] = ((m >= 256 + i) & (m <= 512 + i)).astype(np.float32)
    j = np.arange(32)
    c["CMASK"] = (j[:, None] <= j[None, :]).astype(np.float32)
    d = np.arange(128)[:, None] + 256 - np.arange(384)[None, :]
    c["BMASK"] = np.where((d >= 0) & (d <= 256), 0.0, -1e30).astype(np.float32)
    return c


def rope_tables_l2a(T):
    inv = 1.0 / (10000.0 ** (2.0 * np.arange(32) / 64.0))
    ang = np.arange(T)[:, None] * inv[None, :]
    cos = np.tile(np.cos(ang).T, (4, 1))
    sin = np.tile(np.sin(ang).T, (4, 1))
    return (np.ascontiguousarray(cos).astype(np.float32),
            np.ascontiguousarray(sin).astype(np.float32))


def rope_tables_l3f(T):
    inv = 1.0 / (10000.0 ** (np.arange(0, 64, 2) / 64.0))
    ang = np.arange(T)[:, None] * inv[None, :]
    cos = np.tile(np.cos(ang).T, (4, 1))
    sin = np.tile(np.sin(ang).T, (4, 1))
    return (np.ascontiguousarray(cos).astype(np.float32),
            np.ascontiguousarray(sin).astype(np.float32))


def rope_tables_l2f(T):
    half = np.arange(0, 64, 2, dtype=np.float64)
    inv = 1.0 / (10000.0 ** (half / 64.0))
    t = np.arange(T, dtype=np.float64)
    fr = np.concatenate([t[:, None] * inv[None, :]] * 2, 1)
    base = (half + 0.4 * 64) / (1.4 * 64)
    power = (t - T // 2) / 128.0
    scale = np.concatenate([base[None, :] ** power[:, None]] * 2, 1)
    cos, sin = np.cos(fr), np.sin(fr)
    out = {"L2F_CSQ": (cos * scale).T, "L2F_SNQ": (sin * scale).T,
           "L2F_CSK": (cos / scale).T, "L2F_SNK": (sin / scale).T}
    return {k: np.ascontiguousarray(np.tile(v, (2, 1))).astype(np.float32)
            for k, v in out.items()}


def prep_weights(inputs, T):
    w = {}
    g = lambda k: np.asarray(inputs[k])
    for n in ("wq", "wk", "wv", "wo"):
        w[f"L0_{n}T"] = np.ascontiguousarray(g(f"L0_attn_{n}").T)
    for n in ("bq", "bk", "bv", "bo"):
        w[f"L0_{n}"] = g(f"L0_attn_{n}")
    for n in ("ln_w", "ln_b", "router_w", "gate_w", "up_w", "down_w",
              "sh_gate", "sh_up", "sh_down"):
        w[f"L0_{n}"] = g(f"L0_ffn_{n}")
    for n in ("w_qkv", "b_qkv", "w_gate", "b_gate", "w_proj", "b_proj", "ln_g", "ln_b"):
        w[f"L1_{n}"] = g(f"L1_attn_{n}")
    for n in ("ln_gamma", "ln_beta", "scale", "w1", "w2"):
        w[f"L1F_{n}"] = g(f"L1_ffn_{n}")
    w["L1F_mu"] = np.ascontiguousarray(g("L1_ffn_mu").reshape(D, 256))
    w["L1F_b1T"] = np.ascontiguousarray(g("L1_ffn_b1").T)
    w["L1F_b2exp"] = np.ascontiguousarray(np.repeat(g("L1_ffn_b2").T, 16, axis=1))
    p64 = np.concatenate([np.arange(0, 64, 2), np.arange(1, 64, 2)])
    perm = np.concatenate([h * 64 + p64 for h in range(16)])
    w["L2A_Wq"] = np.ascontiguousarray(g("L2_attn_Wq")[:, perm])
    w["L2A_Wk"] = np.ascontiguousarray(g("L2_attn_Wk")[:, perm])
    w["L2A_bq"] = np.ascontiguousarray(g("L2_attn_bq")[perm])
    w["L2A_bk"] = np.ascontiguousarray(g("L2_attn_bk")[perm])
    w["L2A_Wv"] = g("L2_attn_Wv")
    w["L2A_bv"] = g("L2_attn_bv")
    w["L2A_Wo"] = g("L2_attn_Wo")
    w["L2A_bo"] = g("L2_attn_bo")
    w["L2F_Wqkv"] = g("L2_ffn_Wqkv")
    w["L2F_qs"] = g("L2_ffn_q_scale")
    w["L2F_ks"] = g("L2_ffn_k_scale")
    w["L2F_Wo"] = g("L2_ffn_Wo")
    w["L2F_lng"] = g("L2_ffn_ln_g")
    w["L2F_lnb"] = g("L2_ffn_ln_b")
    W1 = g("L2_ffn_W1")
    b1 = g("L2_ffn_b1")
    IN2 = 2730
    blocks, bblocks = [], []
    for jb in range(22):
        wdt = min(128, IN2 - jb * 128)
        a0 = jb * 128
        blocks += [W1[:, :, a0:a0 + wdt], W1[:, :, IN2 + a0:IN2 + a0 + wdt]]
        bblocks += [b1[:, a0:a0 + wdt], b1[:, IN2 + a0:IN2 + a0 + wdt]]
    w["L2F_W1i"] = np.ascontiguousarray(np.concatenate(blocks, axis=2))
    w["L2F_b1i"] = np.ascontiguousarray(np.concatenate(bblocks, axis=1))
    w["L2F_W2"] = g("L2_ffn_W2")
    W2 = g("L2_ffn_W2")
    W2p = np.zeros((6, 2816, 1024), dtype=np.float32)
    for jb in range(22):
        wdt = min(128, 2730 - jb * 128)
        W2p[:, jb * 128:jb * 128 + wdt, :] = W2[:, jb * 128:jb * 128 + wdt, :]
    w["L2F_W2p"] = W2p
    w["L2F_b2"] = g("L2_ffn_b2")
    for n in ("wq", "bq", "wk", "bk", "wv", "bv", "wo", "bo"):
        w[f"L3A_{n}"] = g(f"L3_attn_{n}")
    for n in ("q_w", "k_w", "v_w"):
        w[f"L3F_{n}T"] = np.ascontiguousarray(g(f"L3_ffn_{n}").T)
    for n in ("q_b", "k_b", "v_b", "gate_b", "ffn_b"):
        w[f"L3F_{n}"] = g(f"L3_ffn_{n}")
    w["L3F_gate_wT"] = np.ascontiguousarray(g("L3_ffn_gate_w").T)
    w["L3F_expert_wT"] = np.ascontiguousarray(np.transpose(g("L3_ffn_expert_w"), (0, 2, 1)))
    w["L3F_expert_b"] = g("L3_ffn_expert_b")
    w["L3F_ffn_wT"] = np.ascontiguousarray(g("L3_ffn_ffn_w").T)
    w.update(make_consts(T))
    w["L2A_COS"], w["L2A_SIN"] = rope_tables_l2a(T)
    w["L3F_COS"], w["L3F_SIN"] = rope_tables_l3f(T)
    w.update(rope_tables_l2f(T))
    return w


LAYER_KEYS = {
    0: ("L0_", "IDENT", "ONES", "AM01", "CMASK"),
    1: ("L1_", "L1F_", "IDENT", "ONES", "AM01"),
    2: ("L2A_", "L2F_", "IDENT", "ONES", "AM01", "BMASK"),
    3: ("L3A_", "L3F_", "IDENT", "ONES", "AM01"),
}


def keys_for_layers(warrs, layers):
    out = []
    for k in warrs:
        for l in layers:
            for p in LAYER_KEYS[l]:
                if k == p or k.startswith(p):
                    out.append(k)
                    break
            else:
                continue
            break
    return sorted(set(out))


class Ctx:
    def __init__(self, kb, din):
        self.kb = kb
        self.din = din
        T = kb.T
        self.ident = kb.sbuf("c_ident", [128, 128], BF16)
        self.ones_b = kb.sbuf("c_ones_b", [128, 128], BF16)
        self.ones_f = kb.sbuf("c_ones_f", [128, 128], F32)
        self.am01 = kb.sbuf("c_am01", [128, 1024], BF16)
        self.at = [kb.sbuf(f"A_t{i}", [128, max(T, 4096)], BF16) for i in range(8)]
        self.stage = [kb.sbuf(f"A_st{i}", [128, FB, 512], BF16) for i in range(3)]
        self.wbuf = [kb.sbuf(f"A_w{i}", [128, FB, 512], BF16) for i in range(2)]
        self.evf = [kb.sbuf(f"A_ev{i}", [128, 512], F32) for i in range(2)]
        self.xrw = [kb.sbuf(f"A_xr{i}", [128, 512], F32) for i in range(4)]
        self.p_sb = [kb.sbuf(f"A_p{i}", [128, 512], BF16) for i in range(6)]
        self.vt = kb.sbuf("A_vt", [128, 32, 132], BF16)
        self.rows = kb.sbuf("A_rows", [1, 6, 512], F32)
        self.bnd = kb.sbuf("A_bnd", [1, 2, 16], F32)
        self.srow = kb.sbuf("A_srow", [128, 16], F32)
        self.sqs = kb.sbuf("A_sqs", [64, 512], BF16)
        self.rbc = kb.sbuf("A_rbc", [64, 512], F32)
        self.rrow = kb.sbuf("A_rrow", [1, 512], F32)
        self.lnb = [kb.sbuf(f"A_ln{i}", [128, T], BF16) for i in range(2)]
        self.pb = [kb.psum(f"pb{i}", [128, 512], F32) for i in range(6)]
        self.pO = kb.psum("pO", [128, 512], F32)
        self.pT = kb.psum("pT", [128, 1024], BF16)

    def load_consts(self):
        kb, din = self.kb, self.din
        kb.dma("gpsimd", self.ident[:], din["IDENT"][:])
        kb.dma("gpsimd", self.ones_b[:], din["ONES"][:])
        kb.dma("gpsimd", self.ones_f[:], din["ONES"][:])
        kb.dma("gpsimd", self.am01[:], din["AM01"][:])
        kb.constcol(LN_EPS, n=1)
        kb.constcol(1.0 / 128.0, dtype=BF16)


def bias_col_load(kb, din_name, din, tag, n_out):
    if n_out <= 128:
        t = kb.sbuf(f"b_{tag}", [n_out, 1], F32)
        kb.dma("gpsimd", t[:], din[din_name].rearrange("(p o) -> p o", o=1))
        return t
    nb = n_out // 128
    t = kb.sbuf(f"b_{tag}", [128, nb], F32)
    r = din[din_name].rearrange("(b p o) -> b p o", b=nb, o=1)
    for b in range(nb):
        kb.dma("gpsimd", t[:, b:b + 1], r[b])
    return t


def bcast_row(kb, ctx, psum_ap, row_ap, nrows=128):
    ones = ctx.ones_f if str(row_ap.tensor.dtype) == str(F32) else ctx.ones_b
    return kb.mm(psum_ap, ones[0:1, 0:nrows], row_ap, True, True)


def hbm_feat(t):
    return t.rearrange("(b p) t -> p b t", p=128)


def ln_pass(kb, ctx, load_tile, dst_hbm, dstkey, T, gamma=None, beta=None,
            persist_bc=False):
    ntt = T // 512
    rows = ctx.rows
    for tt in range(ntt):
        xbf = ctx.stage[tt % 2]
        load_tile(tt, xbf)
        sq = ctx.stage[2]
        kb.act(sq[:], xbf[:], AF.Square)
        for fb in range(FB):
            kb.mm(ctx.pb[0][0:1, :], ctx.ones_b[:, 0:1], xbf[:, fb, :], fb == 0, fb == 7)
            kb.mm(ctx.pb[1][0:1, :], ctx.ones_b[:, 0:1], sq[:, fb, :], fb == 0, fb == 7)
        kb.ts(rows[0:1, 0, :], ctx.pb[0][0:1, :], 1.0 / D, ALU.mult)
        kb.tt(rows[0:1, 1, :], rows[0:1, 0, :], rows[0:1, 0, :], ALU.mult)
        kb.stt(rows[0:1, 2, :], ctx.pb[1][0:1, :], 1.0 / D, rows[0:1, 1, :],
               ALU.mult, ALU.subtract)
        kb.act(rows[0:1, 3, :], rows[0:1, 2, :], AF.Sqrt, bias=kb.constcol(LN_EPS, n=1))
        kb.recip(rows[0:1, 4, :], rows[0:1, 3, :])
        kb.tt(rows[0:1, 5, :], rows[0:1, 0, :], rows[0:1, 4, :], ALU.mult)
        sl = slice(tt * 512, (tt + 1) * 512)
        kb.mm(ctx.pb[2][:, :], ctx.ones_f[0:1, :], rows[0:1, 4, :], True, True)
        kb.mm(ctx.pb[3][:, :], ctx.ones_f[0:1, :], rows[0:1, 5, :], True, True)
        rstd_t, mr_t = ctx.evf[0], ctx.evf[1]
        kb.copy(rstd_t[:], ctx.pb[2][:, :])
        kb.copy(mr_t[:], ctx.pb[3][:, :])
        if persist_bc:
            kb.copy(ctx.lnb[0][:, sl], rstd_t[:])
            kb.copy(ctx.lnb[1][:, sl], mr_t[:])
        xnb = ctx.stage[2]
        for fb in range(FB):
            kb.tt(xnb[:, fb, :], xbf[:, fb, :], rstd_t[:], ALU.mult)
            kb.tt(xnb[:, fb, :], xnb[:, fb, :], mr_t[:], ALU.subtract)
            if gamma is not None:
                kb.ts(xnb[:, fb, :], xnb[:, fb, :], gamma[:, fb:fb + 1], ALU.mult,
                      s2=beta[:, fb:fb + 1], op1=ALU.add)
        kb.dma("sync", hbm_feat(dst_hbm)[:, :, sl], xnb[:], wk=[dstkey])
    kb.merge_writer(dstkey, kb.cur(dstkey, dst_hbm.tensor.name))


def emit_outer_ln(kb, ctx, x_hbm, xkey, xn_hbm):
    xr = hbm_feat(x_hbm)

    def load(tt, buf):
        kb.dma("gpsimd", buf[:], xr[:, :, tt * 512:(tt + 1) * 512], rk=[xkey])
    ln_pass(kb, ctx, load, xn_hbm, "XN", kb.T, persist_bc=True)


class ResAdd:

    def __init__(self, kb, ctx, src_hbm, srckey, dst_hbm, dstkey, extra=None):
        self.kb, self.ctx = kb, ctx
        self.src = hbm_feat(src_hbm)
        self.dst = hbm_feat(dst_hbm)
        self.srckey, self.dstkey = srckey, dstkey
        self.cnt = 0
        self.extra = extra

    def add(self, fo, tt, delta_ap):
        kb, ctx = self.kb, self.ctx
        i = self.cnt % 2
        self.cnt += 1
        rdst = ctx.xrw[i]
        sl = slice(tt * 512, (tt + 1) * 512)
        kb.dma("sync", rdst[:], self.src[:, fo, sl], rk=[f"{self.srckey}@{fo}_{tt}"]
               if self.srckey else [])
        if self.extra is not None:
            self.extra(rdst, fo, tt)
        kb.tt(rdst[:], rdst[:], delta_ap, ALU.add)
        kb.dma("scalar", self.dst[:, fo, sl], rdst[:], wk=[f"{self.dstkey}@{fo}_{tt}"])

    def done(self):
        toks = self.kb.cur(*[f"{self.dstkey}@{fo}_{tt}"
                             for fo in range(FB) for tt in range(self.kb.T // 512)])
        self.kb.merge_writer(self.dstkey, toks)

EMITTERS = {}


def emit_l0_attn(kb, ctx, xn_hbm, a_hbm, vm_hbm, ra, lt="l0a"):
    T = kb.T
    ntt = T // 512
    C = T // 128
    din = ctx.din
    bq = bias_col_load(kb, "L0_bq", din, f"{lt}bq", 1024)
    bk = bias_col_load(kb, "L0_bk", din, f"{lt}bk", 1024)
    bo = bias_col_load(kb, "L0_bo", din, f"{lt}bo", 1024)
    bvrow = kb.sbuf(f"{lt}_bvrow", [1, 1024], F32)
    kb.dma("gpsimd", bvrow[:], din["L0_bv"].rearrange("(o p) -> o p", o=1))
    cmask = kb.sbuf(f"{lt}_cmask", [32, 32], F32)
    kb.dma("gpsimd", cmask[:], din["CMASK"][:])
    qraw = ctx.at[0]
    kraw = ctx.at[1]
    vtok = ctx.vt
    bvbc = kb.sbuf(f"{lt}_bvbc", [128, 128], F32)
    kmean = kb.sbuf(f"{lt}_km", [128, 32], BF16)
    kmf = kb.sbuf(f"{lt}_kmf", [128, 32], F32)
    vmtok = kb.sbuf(f"{lt}_vmtok", [32, 66], BF16)
    vmrow = kb.sbuf(f"{lt}_vmrow", [1, 32, 64], F32)
    rbc2 = kb.sbuf("A_rbc2", [64, 512], F32)
    c128 = kb.constcol(1.0 / 128.0, dtype=BF16)
    brow = ctx.srow[0:1, :]
    bias_sb = kb.sbuf(f"{lt}_bias", [128, 1], F32)
    wqr = hbm_feat(din["L0_wqT"])
    wkr = hbm_feat(din["L0_wkT"])
    wvr = hbm_feat(din["L0_wvT"])
    xr = hbm_feat(xn_hbm)
    kb.memset(vtok[:, :, 64:65], 1.0)
    kb.memset(vtok[:, :, 130:131], 1.0)
    kb.memset(vmtok[:, 64:65], 1.0)
    SC = 0.125
    wpv = kb.sbuf(f"{lt}_wpv", [128, FB, 128], BF16)
    for hp in range(8):
        h0 = 2 * hp
        csl = slice(h0 * 64, (h0 + 2) * 64)
        wpq, wpk = ctx.wbuf[0], ctx.wbuf[1]
        kb.dma("gpsimd", wpq[:, :, 0:128], wqr[:, :, csl])
        kb.dma("gpsimd", wpk[:, :, 0:128], wkr[:, :, csl])
        kb.dma("gpsimd", wpv[:], wvr[:, :, csl])
        bcast_row(kb, ctx, ctx.pb[5][:, 0:128], bvrow[0:1, csl])
        kb.copy(bvbc[:], ctx.pb[5][:, 0:128])
        for tt in range(ntt):
            st = ctx.stage[tt % 2]
            kb.dma("sync", st[:], xr[:, :, tt * 512:(tt + 1) * 512], rk=["XN"])
            tsl = slice(tt * 512, (tt + 1) * 512)
            for fi in range(FB):
                kb.mm(ctx.pb[0][:, :], wpq[:, fi, 0:128], st[:, fi, :], fi == 0, fi == 7)
                kb.mm(ctx.pb[1][:, :], wpk[:, fi, 0:128], st[:, fi, :], fi == 0, fi == 7)
            kb.act(qraw[:, tsl], ctx.pb[0][:, :], AF.Identity, bias=bq[:, hp:hp + 1])
            kb.act(kraw[:, tsl], ctx.pb[1][:, :], AF.Identity, bias=bk[:, hp:hp + 1])
            for t4 in range(4):
                kt = tt * 4 + t4
                for fi in range(FB):
                    kb.mm(ctx.pb[2][:, 0:128], st[:, fi, t4 * 128:(t4 + 1) * 128],
                          wpv[:, fi, :], fi == 0, fi == 7)
                kb.tt(vtok[:, kt, 0:64], ctx.pb[2][:, 0:64], bvbc[:, 0:64], ALU.add)
                kb.tt(vtok[:, kt, 66:130], ctx.pb[2][:, 64:128], bvbc[:, 64:128],
                      ALU.add)
        for hh in range(2):
            base = hh * 64
            voff = hh * 66
            h = h0 + hh
            for which, srcb in enumerate((qraw, kraw)):
                for tt in range(ntt):
                    sl = slice(tt * 512, (tt + 1) * 512)
                    kb.act(ctx.sqs[:], srcb[base:base + 64, sl], AF.Square)
                    kb.mm(ctx.pb[3][0:1, :], ctx.ones_b[0:64, 0:1], ctx.sqs[:],
                          True, True)
                    kb.red(ctx.bnd[0:1, which, tt:tt + 1], ctx.pb[3][0:1, :], ALU.max)
            kb.red(brow[0:1, 0:1], ctx.bnd[0:1, 0, 0:ntt], ALU.max)
            kb.red(brow[0:1, 1:2], ctx.bnd[0:1, 1, 0:ntt], ALU.max)
            kb.tt(brow[0:1, 2:3], brow[0:1, 0:1], brow[0:1, 1:2], ALU.mult)
            kb.act(brow[0:1, 3:4], brow[0:1, 2:3], AF.Sqrt)
            bcast_row(kb, ctx, ctx.pb[3][:, 0:1], brow[0:1, 3:4])
            kb.ts(bias_sb[:], ctx.pb[3][:, 0:1], -SC, ALU.mult)
            kv = kraw[base:base + 64, :].rearrange("p (c t) -> p c t", t=128)
            kb.red(kmf[base:base + 64, 0:C], kv, ALU.add)
            kb.ts(kmean[base:base + 64, 0:C], kmf[base:base + 64, 0:C],
                  1.0 / 128.0, ALU.mult)
            for c in range(C):
                reg = ctx.pb[4][0:1, (c % 8) * 64:(c % 8 + 1) * 64]
                kb.mm(reg, c128, vtok[:, c, voff:voff + 64], True, True)
                kb.copy(vmrow[0:1, c, :], reg)
            vm3 = vm_hbm.rearrange("(o c) f -> o c f", o=1)
            kb.dma("sync", vm3[0:1, 0:C, :], vmrow[0:1, 0:C, :])
            kb.dma("gpsimd", vmtok[0:C, 0:64], vm_hbm[0:C, :])
            for qt in range(ntt):
                qsl = slice(qt * 512, (qt + 1) * 512)
                for c4 in range(4):
                    c = qt * 4 + c4
                    kb.mm(ctx.pb[0][:, c4 * 128:(c4 + 1) * 128],
                          kraw[base:base + 64, c * 128:(c + 1) * 128],
                          qraw[base:base + 64, c * 128:(c + 1) * 128], True, True)
                kb.act(ctx.p_sb[0][:], ctx.pb[0][:, :], AF.Exp, bias=bias_sb[:, 0:1],
                       scale=SC)
                for c4 in range(4):
                    c = qt * 4 + c4
                    kb.mm(ctx.pb[1][0:65, c4 * 128:(c4 + 1) * 128],
                          vtok[:, c, voff:voff + 65],
                          ctx.p_sb[0][:, c4 * 128:(c4 + 1) * 128], True, True)
                kb.mm(ctx.pb[2][0:C, :], kmean[base:base + 64, 0:C],
                      qraw[base:base + 64, qsl], True, True)
                kb.act(ctx.p_sb[1][0:C, :], ctx.pb[2][0:C, :], AF.Exp,
                       bias=bias_sb[0:C, 0:1], scale=SC)
                for c4 in range(4):
                    c = qt * 4 + c4
                    kb.ts(ctx.p_sb[1][0:C, c4 * 128:(c4 + 1) * 128],
                          ctx.p_sb[1][0:C, c4 * 128:(c4 + 1) * 128],
                          cmask[0:C, c:c + 1], ALU.mult)
                kb.mm(ctx.pb[3][0:65, :], vmtok[0:C, 0:65], ctx.p_sb[1][0:C, :],
                      True, True)
                kb.recip(ctx.rrow[:], ctx.pb[1][64:65, :])
                bcast_row(kb, ctx, ctx.pb[4][0:64, :], ctx.rrow[:], nrows=64)
                kb.copy(ctx.rbc[:], ctx.pb[4][0:64, :])
                kb.recip(ctx.rrow[:], ctx.pb[3][64:65, :])
                bcast_row(kb, ctx, ctx.pb[5][0:64, :], ctx.rrow[:], nrows=64)
                kb.copy(rbc2[:], ctx.pb[5][0:64, :])
                kb.tt(ctx.evf[0][0:64, :], ctx.pb[1][0:64, :], ctx.rbc[:], ALU.mult)
                kb.tt(ctx.evf[1][0:64, :], ctx.pb[3][0:64, :], rbc2[:], ALU.mult)
                kb.tt(ctx.p_sb[2][0:64, :], ctx.evf[0][0:64, :], ctx.evf[1][0:64, :],
                      ALU.add)
                kb.dma("sync", a_hbm[h * 64:(h + 1) * 64, qsl], ctx.p_sb[2][0:64, :],
                       wk=["AH"])

    ar = hbm_feat(a_hbm)
    wor = hbm_feat(din["L0_woT"])
    for tt in range(ntt):
        st = ctx.stage[tt % 2]
        kb.dma("sync", st[:], ar[:, :, tt * 512:(tt + 1) * 512], rk=["AH"])
        sl = slice(tt * 512, (tt + 1) * 512)
        for fh in range(2):
            kb.dma("gpsimd", ctx.wbuf[fh][:], wor[:, :, fh * 512:(fh + 1) * 512])
            for fo4 in range(4):
                fo = fh * 4 + fo4
                pb = ctx.pb[fo % 2]
                for fi in range(FB):
                    kb.mm(pb[:, :], ctx.wbuf[fh][:, fi, fo4 * 128:(fo4 + 1) * 128],
                          st[:, fi, :], fi == 0, fi == 7)
                ev = ctx.evf[fo % 2]
                kb.act(ev[:], pb[:, :], AF.Identity, bias=bo[:, fo:fo + 1])
                ra.add(fo, tt, ev[:])
    ra.done()


def emit_l0_ffn(kb, ctx, xn_hbm, xf_hbm, wr_hbm, y_hbm, ra, xorig, xorigkey, lt="l0f"):
    T = kb.T
    ntt = T // 512
    din = ctx.din
    lnw = bias_col_load(kb, "L0_ln_w", din, f"{lt}lnw", 1024)
    lnb = bias_col_load(kb, "L0_ln_b", din, f"{lt}lnb", 1024)
    xr = hbm_feat(xn_hbm)

    def load(tt, buf):
        kb.dma("gpsimd", buf[:], xr[:, :, tt * 512:(tt + 1) * 512], rk=["XN"])
    ln_pass(kb, ctx, load, xf_hbm, "XF", T, gamma=lnw, beta=lnb)

    rw = kb.sbuf(f"{lt}_rw", [128, FB, 8], BF16)
    kb.dma("gpsimd", rw[:], din["L0_router_w"].rearrange("(b p) f -> p b f", p=128))
    gl = kb.sbuf(f"{lt}_gl", [128, 8, 8], F32)
    pe8 = kb.sbuf(f"{lt}_pe8", [128, 8], BF16)
    wrow_bf = ctx.at[0][0:8, :]
    fr = hbm_feat(xf_hbm)
    for tt in range(ntt):
        st = ctx.stage[tt % 2]
        kb.dma("sync", st[:], fr[:, :, tt * 512:(tt + 1) * 512], rk=["XF"])
        for t4 in range(4):
            i128 = tt * 4 + t4
            for fi in range(FB):
                kb.mm(ctx.pb[0][:, 0:8], st[:, fi, t4 * 128:(t4 + 1) * 128],
                      rw[:, fi, :], fi == 0, fi == 7)
            g = gl[:, 0, :]
            kb.copy(g, ctx.pb[0][:, 0:8])
            m1 = gl[:, 4, 0:1]
            kb.red(m1, g, ALU.max)
            sc = gl[:, 4, 1:2]
            kb.act(sc, m1, AF.Sigmoid)
            ind = gl[:, 7, :]
            kb.ts(ind, g, m1, ALU.is_ge)
            kb.ts(pe8[:], ind, sc, ALU.mult)
            kb.tr(ctx.pT[0:8, 0:128], pe8[:], ctx.ident[:])
            kb.copy(wrow_bf[0:8, i128 * 128:(i128 + 1) * 128], ctx.pT[0:8, 0:128])
    kb.dma("sync", wr_hbm[:], wrow_bf[:], wk=["WR0"])

    aw = kb.sbuf(f"{lt}_aw", [128, FB, 512], BF16)
    wbc8 = kb.sbuf(f"{lt}_wbc8", [128, 8, 512], BF16)
    yr = hbm_feat(y_hbm)
    gwr = din["L0_gate_w"]
    uwr = din["L0_up_w"]
    dwr = din["L0_down_w"]
    for tt in range(ntt):
        st2 = ctx.stage[2]
        kb.dma("scalar", st2[:], fr[:, :, tt * 512:(tt + 1) * 512], rk=["XF"])
        src8 = bass.AP(wr_hbm.tensor, tt * 512, [[0, 128], [T, 8], [1, 512]])
        kb.dma("scalar", wbc8[:], src8, rk=["WR0"])
        sl = slice(tt * 512, (tt + 1) * 512)
        for e in range(9):
            if e < 8:
                for fb_ in range(FB):
                    kb.tt(aw[:, fb_, :], st2[:, fb_, :], wbc8[:, e, :], ALU.mult)
                src_aw = aw
                gsrc = gwr[e].rearrange("(b p) f -> p b f", p=128)
                usrc = uwr[e].rearrange("(b p) f -> p b f", p=128)
                dsrc = dwr[e].rearrange("(b p) f -> p b f", p=128)
            else:
                src_aw = st2
                gsrc = din["L0_sh_gate"].rearrange("(b p) f -> p b f", p=128)
                usrc = din["L0_sh_up"].rearrange("(b p) f -> p b f", p=128)
                dsrc = din["L0_sh_down"].rearrange("(b p) f -> p b f", p=128)
            cps = max(1, T // 512)
            hm_slots = [4, 5, 6, 7]
            dw_slots = [6, 7] if cps >= 8 else [1, 2, 3, 0]

            def hm_ap(ic):
                s = hm_slots[ic // cps] if cps < 8 else (4 + ic // 8)
                off = (ic % cps if cps < 8 else ic % 8) * 512
                return ctx.at[s][:, off:off + 512]

            def dw_ap(ic, c0, c1):
                s = dw_slots[ic // cps] if cps < 8 else (6 + ic // 8)
                off = (ic % cps if cps < 8 else ic % 8) * 512
                return ctx.at[s][:, off + c0:off + c1]

            for gh in range(4):
                kb.dma("gpsimd", ctx.wbuf[0][:], gsrc[:, :, gh * 512:(gh + 1) * 512])
                kb.dma("gpsimd", ctx.wbuf[1][:], usrc[:, :, gh * 512:(gh + 1) * 512])
                for ic4 in range(4):
                    ic = gh * 4 + ic4
                    for fi in range(FB):
                        kb.mm(ctx.pb[4][:, :], ctx.wbuf[0][:, fi, ic4 * 128:(ic4 + 1) * 128],
                              src_aw[:, fi, :], fi == 0, fi == 7)
                        kb.mm(ctx.pb[5][:, :], ctx.wbuf[1][:, fi, ic4 * 128:(ic4 + 1) * 128],
                              src_aw[:, fi, :], fi == 0, fi == 7)
                    kb.act(ctx.p_sb[4][:], ctx.pb[4][:, :], AF.Silu)
                    kb.copy(ctx.p_sb[5][:], ctx.pb[5][:, :], eng="scalar")
                    kb.tt(hm_ap(ic), ctx.p_sb[4][:], ctx.p_sb[5][:], ALU.mult)
            dr = dsrc
            nds = 16 // cps if cps < 8 else 2
            for half in range(2):
                for s in range(nds):
                    sl_ic = slice(s * cps if cps < 8 else s * 8,
                                  (s + 1) * cps if cps < 8 else (s + 1) * 8)
                    dst = ctx.at[dw_slots[s] if cps < 8 else 6 + s]
                    kb.dma("gpsimd",
                           dst[:].rearrange("p (b t) -> p b t", t=512),
                           dr[:, sl_ic, half * 512:(half + 1) * 512])
                for fo4 in range(4):
                    fo = half * 4 + fo4
                    for ic in range(16):
                        wap = dw_ap(ic, fo4 * 128, (fo4 + 1) * 128)
                        hap = hm_ap(ic)
                        kb.mm(ctx.pb[fo4][:, :], wap, hap, ic == 0, ic == 15)
                    ev = ctx.evf[fo4 % 2]
                    kb.copy(ev[:], ctx.pb[fo4][:, :], eng="scalar")
                    ykey = f"YH@{tt}_{fo}"
                    if e == 0:
                        kb.dma("gpsimd", yr[:, fo, sl], ev[:], wk=[ykey])
                    else:
                        kb.op("q_gpsimd", "gpsimd",
                              lambda en, o=yr[:, fo, sl], i=ev[:]:
                              en.dma_start(o, i, accum_op=ALU.add),
                              reads=[ev[:], ykey],
                              writes=[yr[:, fo, sl], ykey], inck=16)
    xo = hbm_feat(xorig)

    def extra_xn(rdst, fo, tt):
        sl = slice(tt * 512, (tt + 1) * 512)
        tmp = ctx.xrw[2 + (tt % 2)]
        kb.dma("gpsimd", tmp[:], xo[:, fo, sl], rk=[xorigkey])
        kb.tt(tmp[:], tmp[:], ctx.lnb[0][:, sl], ALU.mult)
        kb.tt(tmp[:], tmp[:], ctx.lnb[1][:, sl], ALU.subtract)
        kb.tt(rdst[:], rdst[:], tmp[:], ALU.add)

    ra.extra = extra_xn
    for tt in range(ntt):
        sl = slice(tt * 512, (tt + 1) * 512)
        for fo in range(FB):
            ev = ctx.evf[fo % 2]
            kb.op("q_sync", "sync",
                  lambda en, o=ev[:], i=yr[:, fo, sl]: en.dma_start(o, i),
                  reads=[yr[:, fo, sl], f"YH@{tt}_{fo}"], writes=[ev[:]], inck=16)
            ra.add(fo, tt, ev[:])
    ra.done()


def emit_layer0(kb, ctx, xin, xinkey, xmid, xout):
    T = kb.T
    nc = kb.nc
    xn = nc.dram_tensor("XN0", [D, T], BF16).ap()
    ah = nc.dram_tensor("AH0", [D, T], BF16).ap()
    xf = nc.dram_tensor("XF0", [D, T], BF16).ap()
    wrh = nc.dram_tensor("WR0", [8, T], BF16).ap()
    yh = nc.dram_tensor("YH0", [D, T], F32).ap()
    vmh = nc.dram_tensor("VM0", [32, 64], F32).ap()
    emit_outer_ln(kb, ctx, xin, xinkey, xn)
    ra = ResAdd(kb, ctx, xin, xinkey, xmid, "XM")
    emit_l0_attn(kb, ctx, xn, ah, vmh, ra)
    ra2 = ResAdd(kb, ctx, xmid, "XM", xout, "XO")
    emit_l0_ffn(kb, ctx, xn, xf, wrh, yh, ra2, xin, xinkey)


EMITTERS[0] = emit_layer0


def emit_l1_attn(kb, ctx, xn_hbm, xn2a_hbm, g_hbm, q_hbm, k_hbm, v_hbm, m_hbm,
                 n_hbm, ra, lt="l1a"):
    T = kb.T
    ntt = T // 512
    C = T // 128
    din = ctx.din
    bqkv = bias_col_load(kb, "L1_b_qkv", din, f"{lt}bqkv", 3072)
    bg = bias_col_load(kb, "L1_b_gate", din, f"{lt}bg", 1024)
    bp = bias_col_load(kb, "L1_b_proj", din, f"{lt}bp", 1024)
    lga = bias_col_load(kb, "L1_ln_g", din, f"{lt}lng", 1024)
    lba = bias_col_load(kb, "L1_ln_b", din, f"{lt}lnb", 1024)
    xnr = hbm_feat(xn_hbm)

    def _load_ln(tt, buf):
        kb.dma("gpsimd", buf[:], xnr[:, :, tt * 512:(tt + 1) * 512], rk=["XN"])
    ln_pass(kb, ctx, _load_ln, xn2a_hbm, "XN2A", T, gamma=lga, beta=lba)
    xr = hbm_feat(xn2a_hbm)
    wqkv = hbm_feat(din["L1_w_qkv"])
    wg = hbm_feat(din["L1_w_gate"])
    wp = hbm_feat(din["L1_w_proj"])
    bvrow = kb.sbuf(f"{lt}_bvrow", [1, 1024], F32)
    r3 = din["L1_b_qkv"].rearrange("(o p) -> o p", o=1)
    kb.dma("gpsimd", bvrow[:], bass.AP(r3.tensor, 2048, [[1, 1], [1, 1024]]))
    bvbc = kb.sbuf(f"{lt}_bvbc", [128, 512], F32)

    def elu1(ap, tmp):
        kb.ts(tmp[:], ap, 0.0, ALU.min)
        kb.act(tmp[:], tmp[:], AF.Exp)
        kb.ts(ap, ap, 0.0, ALU.max)
        kb.tt(ap, ap, tmp[:], ALU.add)

    for tt in range(ntt):
        st = ctx.stage[tt % 2]
        kb.dma("sync", st[:], xr[:, :, tt * 512:(tt + 1) * 512], rk=["XN2A"])
        sl = slice(tt * 512, (tt + 1) * 512)
        for fh in range(2):
            kb.dma("gpsimd", ctx.wbuf[fh][:], wg[:, :, fh * 512:(fh + 1) * 512])
            for fo4 in range(4):
                fo = fh * 4 + fo4
                pb = ctx.pb[fo % 2]
                for fi in range(FB):
                    kb.mm(pb[:, :], ctx.wbuf[fh][:, fi, fo4 * 128:(fo4 + 1) * 128],
                          st[:, fi, :], fi == 0, fi == 7)
                kb.act(ctx.p_sb[fo % 2][:], pb[:, :], AF.Sigmoid, bias=bg[:, fo:fo + 1])
                kb.dma("scalar", hbm_feat(g_hbm)[:, fo, sl], ctx.p_sb[fo % 2][:],
                       wk=["GH"])

    for tt in range(ntt):
        st = ctx.stage[tt % 2]
        kb.dma("sync", st[:], xr[:, :, tt * 512:(tt + 1) * 512], rk=["XN2A"])
        sl = slice(tt * 512, (tt + 1) * 512)
        for which, (dst, base_fo) in enumerate(((q_hbm, 0), (k_hbm, 8))):
            for fh in range(2):
                kb.dma("gpsimd", ctx.wbuf[fh][:],
                       wqkv[:, :, base_fo * 128 + fh * 512:base_fo * 128 + (fh + 1) * 512])
                for fo4 in range(4):
                    fo = fh * 4 + fo4
                    pb = ctx.pb[fo % 2]
                    for fi in range(FB):
                        kb.mm(pb[:, :], ctx.wbuf[fh][:, fi, fo4 * 128:(fo4 + 1) * 128],
                              st[:, fi, :], fi == 0, fi == 7)
                    pp = ctx.p_sb[fo % 2]
                    kb.act(pp[:], pb[:, :], AF.Identity,
                           bias=bqkv[:, base_fo + fo:base_fo + fo + 1])
                    if which == 1:
                        gt = ctx.p_sb[2 + fo % 2]
                        kb.dma("gpsimd", gt[:], hbm_feat(g_hbm)[:, fo, sl], rk=["GH"])
                        kb.tt(pp[:], pp[:], gt[:], ALU.mult)
                    elu1(pp[:], ctx.p_sb[4])
                    kb.dma("scalar", hbm_feat(dst)[:, fo, sl], pp[:],
                           wk=["QH" if which == 0 else "KH"])
        bcast_row(kb, ctx, ctx.pb[4][:, :], bvrow[0:1, 0:512])
        kb.copy(bvbc[:], ctx.pb[4][:, :])
        bcast_row(kb, ctx, ctx.pb[5][:, :], bvrow[0:1, 512:1024])
        kb.copy(ctx.evf[0][:], ctx.pb[5][:, :])
        for t4 in range(4):
            tk = tt * 4 + t4
            for vh in range(2):
                kb.dma("gpsimd", ctx.wbuf[vh][:],
                       wqkv[:, :, 2048 + vh * 512:2048 + (vh + 1) * 512])
                pb = ctx.pb[vh]
                for fi in range(FB):
                    kb.mm(pb[:, :], st[:, fi, t4 * 128:(t4 + 1) * 128],
                          ctx.wbuf[vh][:, fi, :], fi == 0, fi == 7)
                bsrc = bvbc[:] if vh == 0 else ctx.evf[0][:]
                kb.tt(ctx.p_sb[vh][:], pb[:, :], bsrc, ALU.add)
                kb.dma("scalar", v_hbm[tk * 128:(tk + 1) * 128, vh * 512:(vh + 1) * 512],
                       ctx.p_sb[vh][:], wk=["VH"])

    qc = ctx.stage[2]
    kfc = ctx.stage[0]
    ktok = ctx.at[2]
    vc = ctx.at[3]
    mst = [ctx.at[4], ctx.at[5]]
    m2st = kb.sbuf(f"{lt}_m2", [128, FB, 2], BF16)
    upst = kb.sbuf(f"{lt}_upst", [128, 1026], F32)
    kb.memset(upst[:, 1025:1026], 0.0)
    mr_ = hbm_feat(m_hbm)
    kb.memset(vc[:, 1024:1025], 1.0)
    am = ctx.am01[:, 512:640]
    for c in range(C):
        cs = slice(c * 128, (c + 1) * 128)
        kb.dma("sync", qc[:, :, 0:128], hbm_feat(q_hbm)[:, :, cs], rk=["QH"])
        kb.dma("sync", kfc[:, :, 0:128], hbm_feat(k_hbm)[:, :, cs], rk=["KH"])
        kb.dma("sync", vc[:, 0:1024].rearrange("p (b t) -> p b t", t=128),
               v_hbm[cs, :].rearrange("t (b f) -> t b f", f=128), rk=["VH"])
        if c > 0:
            for s in range(2):
                kb.dma("gpsimd", mst[s][:].rearrange("p (b t) -> p b t", t=1024),
                       mr_[:, 4 * s:4 * s + 4, 0:1024], rk=["MH1"])
            kb.dma("gpsimd", m2st[:], mr_[:, :, 1024:1026], rk=["MH1"])
        for fi in range(FB):
            kb.mm(ctx.pb[0][:, 0:128], kfc[:, fi, 0:128], qc[:, fi, 0:128],
                  fi == 0, fi == 7)
        kb.tt(ctx.p_sb[0][:, 0:128], ctx.pb[0][:, 0:128], am, ALU.mult)
        for fi in range(FB):
            kb.tr(ctx.pT[:, 0:128], kfc[:, fi, 0:128], ctx.ident[:])
            kb.copy(ktok[:, fi * 128:(fi + 1) * 128], ctx.pT[:, 0:128])
        kb.mm(ctx.pb[2][0:1, 0:128], vc[:, 1024:1025], ctx.p_sb[0][:, 0:128],
              True, c == 0)
        if c > 0:
            for fi in range(FB):
                kb.mm(ctx.pb[2][0:1, 0:128], m2st[:, fi, 0:1], qc[:, fi, 0:128],
                      False, fi == 7)
        kb.ts(ctx.rrow[0:1, 0:128], ctx.pb[2][0:1, 0:128], 1e-6, ALU.add)
        kb.recip(ctx.rrow[0:1, 0:128], ctx.rrow[0:1, 0:128])
        bcast_row(kb, ctx, ctx.pb[3][:, 0:128], ctx.rrow[0:1, 0:128])
        kb.copy(ctx.evf[1][:, 0:128], ctx.pb[3][:, 0:128])
        for dv in range(FB):
            kb.mm(ctx.pb[1][:, 0:128], vc[:, dv * 128:(dv + 1) * 128],
                  ctx.p_sb[0][:, 0:128], True, c == 0)
            if c > 0:
                for fi in range(FB):
                    ms = mst[fi // 4]
                    kb.mm(ctx.pb[1][:, 0:128],
                          ms[:, (fi % 4) * 1024 + dv * 128:(fi % 4) * 1024 + (dv + 1) * 128],
                          qc[:, fi, 0:128], False, fi == 7)
            kb.tt(ctx.p_sb[1][:, 0:128], ctx.pb[1][:, 0:128], ctx.evf[1][:, 0:128],
                  ALU.mult)
            kb.dma("scalar", hbm_feat(n_hbm)[:, dv, cs], ctx.p_sb[1][:, 0:128],
                   wk=["NH"])
        for dk in range(FB):
            kb.mm(ctx.pb[4][:, :], ktok[:, dk * 128:(dk + 1) * 128],
                  vc[:, 0:512], True, True)
            kb.mm(ctx.pb[5][:, :], ktok[:, dk * 128:(dk + 1) * 128],
                  vc[:, 512:1024], True, True)
            kb.mm(ctx.pO[:, 0:1], ktok[:, dk * 128:(dk + 1) * 128],
                  vc[:, 1024:1025], True, True)
            kb.copy(upst[:, 0:512], ctx.pb[4][:, :], eng="scalar")
            kb.copy(upst[:, 512:1024], ctx.pb[5][:, :], eng="scalar")
            kb.copy(upst[:, 1024:1025], ctx.pO[:, 0:1], eng="scalar")
            mkey = f"MH1@{dk}"
            for (oap, iap) in ((mr_[:, dk, 0:1024], upst[:, 0:1024]),
                               (mr_[:, dk, 1024:1026], upst[:, 1024:1026])):
                if c == 0:
                    kb.op("q_gpsimd", "gpsimd",
                          lambda en, o=oap, i=iap: en.dma_start(o, i),
                          reads=[upst[:]], writes=[oap, mkey, "MH1"], inck=16)
                else:
                    kb.op("q_gpsimd", "gpsimd",
                          lambda en, o=oap, i=iap:
                          en.dma_start(o, i, accum_op=ALU.add),
                          reads=[upst[:], mkey], writes=[oap, mkey, "MH1"], inck=16)

    nr = hbm_feat(n_hbm)
    for tt in range(ntt):
        st = ctx.stage[tt % 2]
        kb.dma("sync", st[:], nr[:, :, tt * 512:(tt + 1) * 512], rk=["NH"])
        sl = slice(tt * 512, (tt + 1) * 512)
        for fh in range(2):
            kb.dma("gpsimd", ctx.wbuf[fh][:], wp[:, :, fh * 512:(fh + 1) * 512])
            for fo4 in range(4):
                fo = fh * 4 + fo4
                pb = ctx.pb[fo % 2]
                for fi in range(FB):
                    kb.mm(pb[:, :], ctx.wbuf[fh][:, fi, fo4 * 128:(fo4 + 1) * 128],
                          st[:, fi, :], fi == 0, fi == 7)
                ev = ctx.evf[fo % 2]
                kb.act(ev[:], pb[:, :], AF.Identity, bias=bp[:, fo:fo + 1])
                ra.add(fo, tt, ev[:])
    ra.done()


def emit_l1_ffn(kb, ctx, xn_hbm, xn2_hbm, xu_hbm, lg_hbm, dt_hbm, cb_hbm, xt_hbm,
                ra, lt="l1f"):
    T = kb.T
    ntt = T // 512
    din = ctx.din
    lng = bias_col_load(kb, "L1F_ln_gamma", din, f"{lt}lng", 1024)
    lnb2 = bias_col_load(kb, "L1F_ln_beta", din, f"{lt}lnb", 1024)
    xr = hbm_feat(xn_hbm)

    def load(tt, buf):
        kb.dma("gpsimd", buf[:], xr[:, :, tt * 512:(tt + 1) * 512], rk=["XN"])
    ln_pass(kb, ctx, load, xn2_hbm, "XN2", T, gamma=lng, beta=lnb2)

    sc1 = kb.sbuf(f"{lt}_sc1", [1, 1], F32)
    kb.dma("gpsimd", sc1[:], din["L1F_scale"].rearrange("(o p) -> o p", o=1))

    x2r = hbm_feat(xn2_hbm)
    for tt in range(ntt):
        st = ctx.stage[tt % 2]
        kb.dma("sync", st[:], x2r[:, :, tt * 512:(tt + 1) * 512], rk=["XN2"])
        sq = ctx.stage[2]
        kb.act(sq[:], st[:], AF.Square)
        for fb in range(FB):
            kb.mm(ctx.pb[0][0:1, :], ctx.ones_b[:, 0:1], sq[:, fb, :], fb == 0, fb == 7)
        rows = ctx.rows
        kb.act(rows[0:1, 0, :], ctx.pb[0][0:1, :], AF.Sqrt)
        kb.ts(rows[0:1, 1, :], rows[0:1, 0, :], 1e-12, ALU.max)
        kb.recip(rows[0:1, 2, :], rows[0:1, 1, :])
        kb.ts(rows[0:1, 3, :], rows[0:1, 2, :], sc1[0:1, 0:1], ALU.mult)
        kb.mm(ctx.pb[1][:, :], ctx.ones_f[0:1, :], rows[0:1, 3, :], True, True)
        kb.copy(ctx.evf[0][:], ctx.pb[1][:, :])
        sl = slice(tt * 512, (tt + 1) * 512)
        for fb in range(FB):
            kb.tt(sq[:, fb, :], st[:, fb, :], ctx.evf[0][:], ALU.mult)
        kb.dma("sync", hbm_feat(xu_hbm)[:, :, sl], sq[:], wk=["XU"])
        for t4 in range(4):
            tc = tt * 4 + t4
            for fb in range(FB):
                kb.tr(ctx.pT[:, 0:128], st[:, fb, t4 * 128:(t4 + 1) * 128], ctx.ident[:])
                kb.copy(ctx.p_sb[fb % 2][:, 0:128], ctx.pT[:, 0:128])
                kb.dma("scalar", xt_hbm[tc * 128:(tc + 1) * 128, fb * 128:(fb + 1) * 128],
                       ctx.p_sb[fb % 2][:, 0:128], wk=["XT1"])

    mu = kb.sbuf(f"{lt}_mu", [128, FB, 256], BF16)
    muf = kb.sbuf(f"{lt}_muf", [128, FB, 256], BF16)
    kb.dma("gpsimd", muf[:], din["L1F_mu"].rearrange("(b p) f -> p b f", p=128))
    sqm = ctx.stage[2]
    kb.act(sqm[:, :, 0:256], muf[:], AF.Square)
    for fb in range(FB):
        kb.mm(ctx.pb[0][0:1, 0:256], ctx.ones_b[:, 0:1], sqm[:, fb, 0:256],
              fb == 0, fb == 7)
    kb.act(ctx.rows[0:1, 0, 0:256], ctx.pb[0][0:1, 0:256], AF.Sqrt)
    kb.ts(ctx.rows[0:1, 1, 0:256], ctx.rows[0:1, 0, 0:256], 1e-12, ALU.max)
    kb.recip(ctx.rows[0:1, 2, 0:256], ctx.rows[0:1, 1, 0:256])
    kb.mm(ctx.pb[1][:, 0:256], ctx.ones_f[0:1, :], ctx.rows[0:1, 2, 0:256], True, True)
    kb.copy(ctx.evf[0][:, 0:256], ctx.pb[1][:, 0:256])
    for fb in range(FB):
        kb.tt(mu[:, fb, :], muf[:, fb, :], ctx.evf[0][:, 0:256], ALU.mult)

    xur = hbm_feat(xu_hbm)
    mdisp = kb.sbuf(f"{lt}_md", [128, 2, 4], F32)
    ldisp = kb.sbuf(f"{lt}_ld", [128, 2, 4], F32)
    for tt in range(ntt):
        st = ctx.stage[tt % 2]
        kb.dma("sync", st[:], xur[:, :, tt * 512:(tt + 1) * 512], rk=["XU"])
        sl = slice(tt * 512, (tt + 1) * 512)
        for eb in range(2):
            pb = ctx.pb[eb]
            for fi in range(FB):
                kb.mm(pb[:, :], mu[:, fi, eb * 128:(eb + 1) * 128], st[:, fi, :],
                      fi == 0, fi == 7)
            kb.copy(ctx.evf[eb][:], pb[:, :])
            kb.dma("scalar", lg_hbm[eb * 128:(eb + 1) * 128, sl], ctx.evf[eb][:],
                   wk=["LG"])
            kb.red(mdisp[:, eb, 1:2], pb[:, :], ALU.max)
            if tt == 0:
                kb.copy(mdisp[:, eb, 0:1], mdisp[:, eb, 1:2])
            else:
                kb.tt(mdisp[:, eb, 0:1], mdisp[:, eb, 0:1], mdisp[:, eb, 1:2], ALU.max)
    for eb in range(2):
        kb.ts(mdisp[:, eb, 3:4], mdisp[:, eb, 0:1], -1.0, ALU.mult)
    for tt in range(ntt):
        sl = slice(tt * 512, (tt + 1) * 512)
        for eb in range(2):
            ef = ctx.evf[eb]
            kb.dma("sync", ef[:], lg_hbm[eb * 128:(eb + 1) * 128, sl], rk=["LG"])
            kb.act(ctx.p_sb[eb][:], ef[:], AF.Exp, bias=mdisp[:, eb, 3:4],
                   accum=ldisp[:, eb, 1:2])
            if tt == 0:
                kb.copy(ldisp[:, eb, 0:1], ldisp[:, eb, 1:2])
            else:
                kb.tt(ldisp[:, eb, 0:1], ldisp[:, eb, 0:1], ldisp[:, eb, 1:2], ALU.add)
    for eb in range(2):
        kb.recip(ldisp[:, eb, 2:3], ldisp[:, eb, 0:1])

    cbq = kb.sbuf(f"{lt}_cbq", [128, 2, 128], BF16)
    ctq = kb.sbuf(f"{lt}_ctq", [128, 256], F32)
    crow = kb.sbuf(f"{lt}_crow", [128, 4], F32)
    for tc in range(T // 128):
        cs = slice(tc * 128, (tc + 1) * 128)
        for eb in range(2):
            ef = ctx.evf[eb]
            kb.dma("sync", ef[:, 0:128], lg_hbm[eb * 128:(eb + 1) * 128, cs], rk=["LG"])
            kb.act(ctx.p_sb[eb][:, 0:128], ef[:, 0:128], AF.Exp, bias=mdisp[:, eb, 3:4])
            kb.ts(ctx.p_sb[eb][:, 0:128], ctx.p_sb[eb][:, 0:128], ldisp[:, eb, 2:3],
                  ALU.mult)
            kb.tr(ctx.pT[:, 0:128], ctx.p_sb[eb][:, 0:128], ctx.ident[:])
            kb.copy(cbq[:, eb, :], ctx.pT[:, 0:128])
        kb.dma("scalar", dt_hbm[cs, :].rearrange("t (b f) -> t b f", f=128),
               cbq[:], wk=["DT"])
        for eb in range(2):
            bf_ = ctx.p_sb[2 + eb]
            kb.copy(bf_[:, 0:128], ctx.evf[eb][:, 0:128])
            kb.tr(ctx.pT[:, 0:128], bf_[:, 0:128], ctx.ident[:])
            kb.copy(ctq[:, eb * 128:(eb + 1) * 128], ctx.pT[:, 0:128])
        kb.red(crow[:, 0:1], ctq[:], ALU.max)
        kb.ts(crow[:, 1:2], crow[:, 0:1], -1.0, ALU.mult)
        kb.act(ctq[:], ctq[:], AF.Exp, bias=crow[:, 1:2], accum=crow[:, 2:3])
        kb.recip(crow[:, 3:4], crow[:, 2:3])
        kb.ts(cbq[:, 0, :], ctq[:, 0:128], crow[:, 3:4], ALU.mult)
        kb.ts(cbq[:, 1, :], ctq[:, 128:256], crow[:, 3:4], ALU.mult)
        for eb in range(2):
            kb.tr(ctx.pT[:, 0:128], cbq[:, eb, :], ctx.ident[:])
            kb.copy(ctx.p_sb[4][:, 0:128], ctx.pT[:, 0:128])
            kb.dma("scalar", cb_hbm[eb * 128:(eb + 1) * 128, cs],
                   ctx.p_sb[4][:, 0:128], wk=["CB"])

    si = kb.sbuf(f"{lt}_si", [128, FB, 256], BF16)
    for db in range(FB):
        for tc in range(T // 128):
            xtile = ctx.p_sb[tc % 2]
            kb.dma("sync", xtile[:, 0:128],
                   xt_hbm[tc * 128:(tc + 1) * 128, db * 128:(db + 1) * 128],
                   rk=["XT1"])
            dtile = ctx.p_sb[2 + tc % 2]
            kb.dma("scalar", dtile[:, 0:256],
                   dt_hbm[tc * 128:(tc + 1) * 128, :], rk=["DT"])
            kb.mm(ctx.pb[2][:, 0:256], xtile[:, 0:128], dtile[:, 0:256],
                  tc == 0, tc == T // 128 - 1)
        kb.copy(si[:, db, :], ctx.pb[2][:, 0:256])

    w1r = din["L1F_w1"]
    w2r = din["L1F_w2"]
    b1t = kb.sbuf(f"{lt}_b1t", [128, 2, 16], F32)
    kb.dma("gpsimd", b1t[:], din["L1F_b1T"].rearrange("(b p) e -> p b e", p=128))
    b2e = kb.sbuf(f"{lt}_b2e", [128, FB, 256], BF16)
    kb.dma("gpsimd", b2e[:], din["L1F_b2exp"].rearrange("(b p) f -> p b f", p=128))
    hsb = kb.sbuf(f"{lt}_h", [128, 2, 16], BF16)
    so = kb.sbuf(f"{lt}_so", [128, FB, 256], BF16)
    w1b = kb.sbuf(f"{lt}_w1b", [128, FB, 256], BF16)
    w2b = kb.sbuf(f"{lt}_w2b", [128, 2, 1024], BF16)
    for e in range(16):
        kb.dma("gpsimd", w1b[:], w1r[e].rearrange("(b p) f -> p b f", p=128))
        kb.dma("gpsimd", w2b[:], w2r[e].rearrange("(b p) f -> p b f", p=128))
        esl = slice(e * 16, (e + 1) * 16)
        for hb in range(2):
            for fi in range(FB):
                kb.mm(ctx.pb[3][:, 0:16], w1b[:, fi, hb * 128:(hb + 1) * 128],
                      si[:, fi, esl], fi == 0, fi == 7)
            kb.act(hsb[:, hb, :], ctx.pb[3][:, 0:16], AF.Gelu, bias=b1t[:, hb, e:e + 1])
        for db in range(FB):
            for hb in range(2):
                kb.mm(ctx.pb[4][:, 0:16], w2b[:, hb, db * 128:(db + 1) * 128],
                      hsb[:, hb, :], hb == 0, hb == 1)
            kb.tt(so[:, db, esl], ctx.pb[4][:, 0:16], b2e[:, db, esl], ALU.add)

    sot = [ctx.at[6], ctx.at[7]]
    for fb in range(FB):
        for eb in range(2):
            kb.tr(ctx.pT[:, 0:128], so[:, fb, eb * 128:(eb + 1) * 128], ctx.ident[:])
            kb.copy(sot[eb][:, fb * 128:(fb + 1) * 128], ctx.pT[:, 0:128])
    for tt in range(ntt):
        sl = slice(tt * 512, (tt + 1) * 512)
        cbt = [ctx.p_sb[0], ctx.p_sb[1]]
        for eb in range(2):
            kb.dma("sync", cbt[eb][:], cb_hbm[eb * 128:(eb + 1) * 128, sl], rk=["CB"])
        for fo in range(FB):
            pb = ctx.pb[fo % 2]
            for eb in range(2):
                kb.mm(pb[:, :], sot[eb][:, fo * 128:(fo + 1) * 128], cbt[eb][:],
                      eb == 0, eb == 1)
            ev = ctx.evf[fo % 2]
            kb.copy(ev[:], pb[:, :], eng="scalar")
            ra.add(fo, tt, ev[:])
    ra.done()


def emit_layer1(kb, ctx, xin, xinkey, xmid, xout):
    T = kb.T
    nc = kb.nc
    xn = nc.dram_tensor("XN1", [D, T], BF16).ap()
    gh = nc.dram_tensor("GH1", [D, T], BF16).ap()
    qh = nc.dram_tensor("QH1", [D, T], BF16).ap()
    kh = nc.dram_tensor("KH1", [D, T], BF16).ap()
    vh = nc.dram_tensor("VH1", [T, D], BF16).ap()
    mh = nc.dram_tensor("MH1", [D, 1026], F32).ap()
    nh = nc.dram_tensor("NH1", [D, T], BF16).ap()
    xn2 = nc.dram_tensor("XN21", [D, T], BF16).ap()
    xu = nc.dram_tensor("XU1", [D, T], BF16).ap()
    lg = nc.dram_tensor("LG1", [256, T], F32).ap()
    dt = nc.dram_tensor("DT1", [T, 256], BF16).ap()
    cb = nc.dram_tensor("CB1", [256, T], BF16).ap()
    xt = nc.dram_tensor("XT1", [T, D], BF16).ap()
    xn2a = nc.dram_tensor("XN2A1", [D, T], BF16).ap()
    emit_outer_ln(kb, ctx, xin, xinkey, xn)
    ra = ResAdd(kb, ctx, xin, xinkey, xmid, "XM")
    emit_l1_attn(kb, ctx, xn, xn2a, gh, qh, kh, vh, mh, nh, ra)
    ra2 = ResAdd(kb, ctx, xmid, "XM", xout, "XO")
    emit_l1_ffn(kb, ctx, xn, xn2, xu, lg, dt, cb, xt, ra2)


EMITTERS[1] = emit_layer1


def emit_l2_attn(kb, ctx, xn_hbm, a_hbm, ra, lt="l2a"):
    T = kb.T
    ntt = T // 512
    nkt = T // 128
    din = ctx.din
    bq = bias_col_load(kb, "L2A_bq", din, f"{lt}bq", 1024)
    bk = bias_col_load(kb, "L2A_bk", din, f"{lt}bk", 1024)
    bo = bias_col_load(kb, "L2A_bo", din, f"{lt}bo", 1024)
    bvrow = kb.sbuf(f"{lt}_bvrow", [1, 1024], F32)
    kb.dma("gpsimd", bvrow[:], din["L2A_bv"].rearrange("(o p) -> o p", o=1))
    cos_sb = ctx.at[4][:, 0:T]
    sin_sb = ctx.at[5][:, 0:T]
    kb.dma("gpsimd", cos_sb, din["L2A_COS"][:])
    kb.dma("gpsimd", sin_sb, din["L2A_SIN"][:])
    qraw = ctx.at[0]
    kraw = ctx.at[1]
    kh = [ctx.at[2][0:64, :], ctx.at[3][0:64, :]]
    qh = [ctx.at[6][0:64, :], ctx.at[7][0:64, :]]
    vtok = ctx.vt
    rsc = kb.sbuf(f"{lt}_rsc", [64, max(kb.T, 4096)], BF16)
    brow = ctx.srow[0:1, :]
    bias_sb = kb.sbuf(f"{lt}_bias", [128, 2], F32)
    bvbc = kb.sbuf(f"{lt}_bvbc", [128, 128], F32)
    kb.memset(vtok[:, :, 64:65], 1.0)
    kb.memset(vtok[:, :, 130:131], 1.0)
    xr = hbm_feat(xn_hbm)
    SC = 0.125

    def rope(raw, base, out):
        kb.tt(out[0:32, 0:T], raw[base:base + 32, 0:T],
              cos_sb[base:base + 32, 0:T], ALU.mult)
        kb.tt(rsc[0:32, 0:T], raw[base + 32:base + 64, 0:T],
              sin_sb[base + 32:base + 64, 0:T], ALU.mult)
        kb.tt(out[0:32, 0:T], out[0:32, 0:T], rsc[0:32, 0:T], ALU.subtract)
        kb.tt(out[32:64, 0:T], raw[base + 32:base + 64, 0:T],
              cos_sb[base + 32:base + 64, 0:T], ALU.mult)
        kb.tt(rsc[32:64, 0:T], raw[base:base + 32, 0:T],
              sin_sb[base:base + 32, 0:T], ALU.mult)
        kb.tt(out[32:64, 0:T], out[32:64, 0:T], rsc[32:64, 0:T], ALU.add)

    wqr = hbm_feat(din["L2A_Wq"])
    wkr = hbm_feat(din["L2A_Wk"])
    wvr = hbm_feat(din["L2A_Wv"])
    wpv = kb.sbuf(f"{lt}_wpv", [128, FB, 128], BF16)
    for hp in range(8):
        h0 = 2 * hp
        csl = slice(h0 * 64, (h0 + 2) * 64)
        wpq, wpk = ctx.wbuf[0], ctx.wbuf[1]
        kb.dma("gpsimd", wpq[:, :, 0:128], wqr[:, :, csl])
        kb.dma("gpsimd", wpk[:, :, 0:128], wkr[:, :, csl])
        kb.dma("gpsimd", wpv[:], wvr[:, :, csl])
        bcast_row(kb, ctx, ctx.pb[5][:, 0:128], bvrow[0:1, csl])
        kb.copy(bvbc[:], ctx.pb[5][:, 0:128])
        for tt in range(ntt):
            st = ctx.stage[tt % 2]
            kb.dma("sync", st[:], xr[:, :, tt * 512:(tt + 1) * 512], rk=["XN"])
            tsl = slice(tt * 512, (tt + 1) * 512)
            for fi in range(FB):
                kb.mm(ctx.pb[0][:, :], wpq[:, fi, 0:128], st[:, fi, :], fi == 0, fi == 7)
                kb.mm(ctx.pb[1][:, :], wpk[:, fi, 0:128], st[:, fi, :], fi == 0, fi == 7)
            kb.act(qraw[:, tsl], ctx.pb[0][:, :], AF.Identity, bias=bq[:, hp:hp + 1])
            kb.act(kraw[:, tsl], ctx.pb[1][:, :], AF.Identity, bias=bk[:, hp:hp + 1])
            for t4 in range(4):
                kt = tt * 4 + t4
                for fi in range(FB):
                    kb.mm(ctx.pb[2][:, 0:128], st[:, fi, t4 * 128:(t4 + 1) * 128],
                          wpv[:, fi, :], fi == 0, fi == 7)
                kb.tt(vtok[:, kt, 0:64], ctx.pb[2][:, 0:64], bvbc[:, 0:64], ALU.add)
                kb.tt(vtok[:, kt, 66:130], ctx.pb[2][:, 64:128], bvbc[:, 64:128],
                      ALU.add)
        for hh in range(2):
            rope(qraw, hh * 64, qh[hh])
            rope(kraw, hh * 64, kh[hh])
        for hh in range(2):
            for which, srcb in enumerate((qh[hh], kh[hh])):
                for tt in range(ntt):
                    sl = slice(tt * 512, (tt + 1) * 512)
                    kb.act(ctx.sqs[:], srcb[:, sl], AF.Square)
                    kb.mm(ctx.pb[3][0:1, :], ctx.ones_b[0:64, 0:1], ctx.sqs[:],
                          True, True)
                    kb.red(ctx.bnd[0:1, which, tt:tt + 1], ctx.pb[3][0:1, :], ALU.max)
            kb.red(brow[0:1, 0:1], ctx.bnd[0:1, 0, 0:ntt], ALU.max)
            kb.red(brow[0:1, 1:2], ctx.bnd[0:1, 1, 0:ntt], ALU.max)
            kb.tt(brow[0:1, 2:3], brow[0:1, 0:1], brow[0:1, 1:2], ALU.mult)
            kb.act(brow[0:1, 3:4], brow[0:1, 2:3], AF.Sqrt)
            bcast_row(kb, ctx, ctx.pb[3][:, 0:1], brow[0:1, 3:4])
            kb.ts(bias_sb[:, hh:hh + 1], ctx.pb[3][:, 0:1], -SC, ALU.mult)
        for hh in range(2):
            voff = hh * 66
            for qt in range(ntt):
                qsl = slice(qt * 512, (qt + 1) * 512)
                nk = 4 * qt + 4
                for g0 in range(0, nk, 6):
                    kts = list(range(g0, min(g0 + 6, nk)))
                    for j, kt in enumerate(kts):
                        kb.mm(ctx.pb[j][:, :], kh[hh][:, kt * 128:(kt + 1) * 128],
                              qh[hh][:, qsl], True, True)
                    for j, kt in enumerate(kts):
                        kb.act(ctx.p_sb[j][:], ctx.pb[j][:], AF.Exp,
                               bias=bias_sb[:, hh:hh + 1], scale=SC)
                        if kt >= 4 * qt:
                            jr = kt - 4 * qt
                            kb.tt(ctx.p_sb[j][:], ctx.p_sb[j][:],
                                  ctx.am01[:, 512 - 128 * jr:1024 - 128 * jr],
                                  ALU.mult)
                    for j, kt in enumerate(kts):
                        kb.mm(ctx.pO[0:65, :], vtok[:, kt, voff:voff + 65],
                              ctx.p_sb[j][:], kt == 0, kt == nk - 1)
                kb.recip(ctx.rrow[:], ctx.pO[64:65, :])
                bcast_row(kb, ctx, ctx.pb[0][0:64, :], ctx.rrow[:], nrows=64)
                kb.copy(ctx.rbc[:], ctx.pb[0][0:64, :])
                kb.tt(ctx.p_sb[0][0:64, :], ctx.pO[0:64, :], ctx.rbc[:], ALU.mult)
                h = h0 + hh
                kb.dma("sync", a_hbm[h * 64:(h + 1) * 64, qsl],
                       ctx.p_sb[0][0:64, :], wk=["AH"])

    ar = hbm_feat(a_hbm)
    wor = hbm_feat(din["L2A_Wo"])
    for tt in range(ntt):
        st = ctx.stage[tt % 2]
        kb.dma("sync", st[:], ar[:, :, tt * 512:(tt + 1) * 512], rk=["AH"])
        sl = slice(tt * 512, (tt + 1) * 512)
        for fh in range(2):
            kb.dma("gpsimd", ctx.wbuf[fh][:], wor[:, :, fh * 512:(fh + 1) * 512])
            for fo4 in range(4):
                fo = fh * 4 + fo4
                pb = ctx.pb[fo % 2]
                for fi in range(FB):
                    kb.mm(pb[:, :], ctx.wbuf[fh][:, fi, fo4 * 128:(fo4 + 1) * 128],
                          st[:, fi, :], fi == 0, fi == 7)
                ev = ctx.evf[fo % 2]
                kb.act(ev[:], pb[:, :], AF.Identity, bias=bo[:, fo:fo + 1])
                ra.add(fo, tt, ev[:])
    ra.done()


def emit_l2_ffn(kb, ctx, xn_hbm, x2_hbm, a2_hbm, xf2_hbm, hm_hbm, ra, lt="l2f"):
    T = kb.T
    ntt = T // 512
    nkt = T // 128
    nw = T // 256
    din = ctx.din
    kb.dma("gpsimd", hbm_feat(x2_hbm), hbm_feat(xn_hbm), rk=["XN"], wk=["X2"])
    bmask = kb.sbuf(f"{lt}_bmask", [128, 384], F32)
    kb.dma("gpsimd", bmask[:], din["BMASK"][:])
    kb.memset(ctx.p_sb[5][:], 0.0)
    for tt in range(ntt):
        kb.dma("scalar", hm_hbm[2730:2816, tt * 512:(tt + 1) * 512],
               ctx.p_sb[5][0:86, :], wk=["HM2"])
    x2r = hbm_feat(x2_hbm)
    IN2 = 2730
    W1BLK = []
    off = 0
    for jb in range(22):
        wdt = min(128, IN2 - jb * 128)
        W1BLK.append((off, wdt))
        off += 2 * wdt

    def shift_load(tt, buf, q="gpsimd"):
        sl = slice(tt * 512, (tt + 1) * 512)
        kb.dma(q, buf[:, 0:4, :], x2r[:, 0:4, sl], rk=["X2"])
        if tt == 0:
            kb.memset(buf[:, 4:8, 0:1], 0.0)
            kb.dma(q, buf[:, 4:8, 1:512],
                   bass.AP(x2r.tensor, 4 * 128 * T, [[T, 128], [128 * T, 4], [1, 511]]),
                   rk=["X2"])
        else:
            kb.dma(q, buf[:, 4:8, :],
                   bass.AP(x2r.tensor, 4 * 128 * T + tt * 512 - 1,
                           [[T, 128], [128 * T, 4], [1, 512]]), rk=["X2"])

    for li in range(6):
        wqkv = hbm_feat(din["L2F_Wqkv"][li])
        woin = hbm_feat(din["L2F_Wo"][li])
        qsc = kb.sbuf(f"{lt}_qs", [128, 1], F32)
        ksc = kb.sbuf(f"{lt}_ks", [128, 1], F32)
        q6 = din["L2F_qs"].rearrange("l (p o) -> l p o", o=1)
        k6 = din["L2F_ks"].rearrange("l (p o) -> l p o", o=1)
        kb.dma("gpsimd", qsc[0:64, :], q6[li])
        kb.dma("gpsimd", qsc[64:128, :], q6[li])
        kb.dma("gpsimd", ksc[0:64, :], k6[li])
        kb.dma("gpsimd", ksc[64:128, :], k6[li])
        cstbl = ctx.at[4][:, 0:T]
        sntbl = ctx.at[5][:, 0:T]
        qraw = ctx.at[0]
        kraw = ctx.at[1]
        kh = [ctx.at[2][0:64, :], ctx.at[3][0:64, :]]
        qh = [ctx.at[6][0:64, :], ctx.at[7][0:64, :]]
        rsc = kb.sbuf(f"{lt}_rsc", [64, max(kb.T, 4096)], BF16)
        vtok = ctx.vt
        kb.memset(vtok[:, :, 64:65], 1.0)
        kb.memset(vtok[:, :, 130:131], 1.0)
        bias_sb = kb.sbuf(f"{lt}_bias", [128, 32], F32)
        wpv = kb.sbuf(f"{lt}_wpv", [128, FB, 128], BF16)
        SC8 = 8.0

        def ropex_inplace(out, cs, sn):
            kb.tt(rsc[0:32, 0:T], out[32:64, 0:T], sn[32:64, 0:T], ALU.mult)
            kb.tt(rsc[32:64, 0:T], out[0:32, 0:T], sn[0:32, 0:T], ALU.mult)
            kb.tt(out[0:32, 0:T], out[0:32, 0:T], cs[0:32, 0:T], ALU.mult)
            kb.tt(out[0:32, 0:T], out[0:32, 0:T], rsc[0:32, 0:T], ALU.subtract)
            kb.tt(out[32:64, 0:T], out[32:64, 0:T], cs[32:64, 0:T], ALU.mult)
            kb.tt(out[32:64, 0:T], out[32:64, 0:T], rsc[32:64, 0:T], ALU.add)

        for hp in range(8):
            h0 = 2 * hp
            wpq, wpk = ctx.wbuf[0], ctx.wbuf[1]
            kb.dma("gpsimd", wpq[:, :, 0:128], wqkv[:, :, h0 * 64:(h0 + 2) * 64])
            kb.dma("gpsimd", wpk[:, :, 0:128],
                   wqkv[:, :, 1024 + h0 * 64:1024 + (h0 + 2) * 64])
            kb.dma("gpsimd", wpv[:], wqkv[:, :, 2048 + h0 * 64:2048 + (h0 + 2) * 64])
            for tt in range(ntt):
                st = ctx.stage[tt % 2]
                shift_load(tt, st, q="gpsimd")
                tsl = slice(tt * 512, (tt + 1) * 512)
                for fi in range(FB):
                    kb.mm(ctx.pb[0][:, :], wpq[:, fi, 0:128], st[:, fi, :],
                          fi == 0, fi == 7)
                    kb.mm(ctx.pb[1][:, :], wpk[:, fi, 0:128], st[:, fi, :],
                          fi == 0, fi == 7)
                kb.copy(qraw[:, tsl], ctx.pb[0][:, :], eng="scalar")
                kb.copy(kraw[:, tsl], ctx.pb[1][:, :], eng="scalar")
                for t4 in range(4):
                    kt = tt * 4 + t4
                    for fi in range(FB):
                        kb.mm(ctx.pb[2][:, 0:128], st[:, fi, t4 * 128:(t4 + 1) * 128],
                              wpv[:, fi, :], fi == 0, fi == 7)
                    kb.copy(vtok[:, kt, 0:64], ctx.pb[2][:, 0:64])
                    kb.copy(vtok[:, kt, 66:130], ctx.pb[2][:, 64:128])
            for hh2 in range(2):
                tq = ("L2F_CSQ", "L2F_SNQ") if hh2 == 0 else ("L2F_CSK", "L2F_SNK")
                kb.dma("gpsimd", cstbl, din[tq[0]][:, 0:T])
                kb.dma("gpsimd", sntbl, din[tq[1]][:, 0:T])
                for hh in range(2):
                    base = hh * 64
                    raw = qraw if hh2 == 0 else kraw
                    out = qh[hh] if hh2 == 0 else kh[hh]
                    scl = qsc if hh2 == 0 else ksc
                    cs, sn = cstbl, sntbl
                    for tt in range(ntt):
                        sl = slice(tt * 512, (tt + 1) * 512)
                        kb.act(ctx.sqs[:], raw[base:base + 64, sl], AF.Square)
                        kb.mm(ctx.pb[3][0:1, :], ctx.ones_b[0:64, 0:1], ctx.sqs[:],
                              True, True)
                        rows = ctx.rows
                        kb.act(rows[0:1, 0, :], ctx.pb[3][0:1, :], AF.Sqrt)
                        kb.ts(rows[0:1, 1, :], rows[0:1, 0, :], 1e-12, ALU.max)
                        kb.recip(rows[0:1, 2, :], rows[0:1, 1, :])
                        kb.mm(ctx.pb[4][0:64, :], ctx.ones_f[0:1, 0:64],
                              rows[0:1, 2, :], True, True)
                        kb.copy(ctx.rbc[:], ctx.pb[4][0:64, :])
                        kb.ts(out[:, sl], raw[base:base + 64, sl],
                              scl[base:base + 64, 0:1], ALU.mult)
                        kb.tt(out[:, sl], out[:, sl], ctx.rbc[:], ALU.mult)
                    ropex_inplace(out, cs, sn)
            for hh in range(2):
                voff = hh * 66
                for qt in range(nkt):
                    nk0 = min(qt, 2)
                    klen = 128 * (nk0 + 1)
                    kst = 128 * (qt - nk0)
                    pbs = ctx.pb[qt % 2]
                    kb.mm(pbs[:, 0:klen], qh[hh][:, qt * 128:(qt + 1) * 128],
                          kh[hh][:, kst:kst + klen], True, True)
                    msk = ctx.evf[qt % 2]
                    mx = ctx.srow[:, 0:1]
                    kb.tt(msk[:, 0:klen], pbs[:, 0:klen], bmask[:, 384 - klen:384],
                          ALU.add)
                    kb.red(mx, msk[:, 0:klen], ALU.max)
                    nm = ctx.srow[:, 1:2]
                    kb.ts(nm, mx, -8.0, ALU.mult)
                    pp = ctx.p_sb[qt % 2]
                    lsum = ctx.srow[:, 2:3]
                    kb.act(pp[:, 0:klen], msk[:, 0:klen], AF.Exp, bias=nm,
                           scale=8.0, accum=lsum)
                    rl = ctx.srow[:, 3:4]
                    kb.recip(rl, lsum)
                    kb.ts(pp[:, 0:klen], pp[:, 0:klen], rl, ALU.mult)
                    for j in range(nk0 + 1):
                        kb.tr(ctx.pT[:, j * 128:(j + 1) * 128],
                              pp[:, j * 128:(j + 1) * 128], ctx.ident[:])
                    pts = ctx.p_sb[2 + qt % 2]
                    kb.copy(pts[:, 0:klen], ctx.pT[:, 0:klen])
                    for j in range(nk0 + 1):
                        kt = qt - nk0 + j
                        kb.mm(ctx.pO[0:64, 0:128], vtok[:, kt, voff:voff + 64],
                              pts[:, j * 128:(j + 1) * 128], j == 0, j == nk0)
                    kb.copy(qh[hh][:, qt * 128:(qt + 1) * 128], ctx.pO[0:64, 0:128],
                            eng="scalar")
                h = h0 + hh
                kb.dma("sync", a2_hbm[h * 64:(h + 1) * 64, 0:T], qh[hh][0:64, 0:T],
                       wk=["A2"])
                ra_a = ResAdd(kb, ctx, x2_hbm, "X2", x2_hbm, "X2")
        ar2 = hbm_feat(a2_hbm)
        for tt in range(ntt):
            st = ctx.stage[tt % 2]
            kb.dma("sync", st[:], ar2[:, :, tt * 512:(tt + 1) * 512], rk=["A2"])
            sl = slice(tt * 512, (tt + 1) * 512)
            for fh in range(2):
                kb.dma("gpsimd", ctx.wbuf[fh][:], woin[:, :, fh * 512:(fh + 1) * 512])
                for fo4 in range(4):
                    fo = fh * 4 + fo4
                    pb = ctx.pb[fo % 2]
                    for fi in range(FB):
                        kb.mm(pb[:, :], ctx.wbuf[fh][:, fi, fo4 * 128:(fo4 + 1) * 128],
                              st[:, fi, :], fi == 0, fi == 7)
                    ev = ctx.evf[fo % 2]
                    kb.copy(ev[:], pb[:, :], eng="scalar")
                    ra_a.add(fo, tt, ev[:])
        ra_a.done()

        lngam = kb.sbuf(f"{lt}_lng", [128, FB], F32)
        lnbet = kb.sbuf(f"{lt}_lnb", [128, FB], F32)
        g6 = din["L2F_lng"].rearrange("l (b p o) -> l b p o", b=FB, o=1)
        b6 = din["L2F_lnb"].rearrange("l (b p o) -> l b p o", b=FB, o=1)
        for b in range(FB):
            kb.dma("gpsimd", lngam[:, b:b + 1], g6[li, b])
            kb.dma("gpsimd", lnbet[:, b:b + 1], b6[li, b])

        def load2(tt, buf):
            shift_load(tt, buf, q="gpsimd")
        ln_pass(kb, ctx, load2, xf2_hbm, "XF2", T, gamma=lngam, beta=lnbet)

        f2r = hbm_feat(xf2_hbm)
        w1r = hbm_feat(din["L2F_W1i"][li])
        b1base = din["L2F_b1i"].tensor
        b1sb = kb.sbuf(f"{lt}_b1", [128, 44], F32)
        for jb in range(22):
            o2, wdt = W1BLK[jb]
            kb.dma("gpsimd", b1sb[0:wdt, 2 * jb:2 * jb + 1],
                   bass.AP(b1base, li * 5460 + o2, [[1, wdt], [1, 1]]))
            kb.dma("gpsimd", b1sb[0:wdt, 2 * jb + 1:2 * jb + 2],
                   bass.AP(b1base, li * 5460 + o2 + wdt, [[1, wdt], [1, 1]]))
        for tt in range(ntt):
            st = ctx.stage[tt % 2]
            kb.dma("sync", st[:], f2r[:, :, tt * 512:(tt + 1) * 512], rk=["XF2"])
            sl = slice(tt * 512, (tt + 1) * 512)
            for jb in range(22):
                o2, wdt = W1BLK[jb]
                kb.dma("gpsimd", ctx.wbuf[jb % 2][:, :, 0:2 * wdt],
                       w1r[:, :, o2:o2 + 2 * wdt])
                for fi in range(FB):
                    kb.mm(ctx.pb[0][0:wdt, :], ctx.wbuf[jb % 2][:, fi, 0:wdt],
                          st[:, fi, :], fi == 0, fi == 7)
                    kb.mm(ctx.pb[1][0:wdt, :], ctx.wbuf[jb % 2][:, fi, wdt:2 * wdt],
                          st[:, fi, :], fi == 0, fi == 7)
                kb.act(ctx.p_sb[0][0:wdt, :], ctx.pb[0][0:wdt, :], AF.Identity,
                       bias=b1sb[0:wdt, 2 * jb:2 * jb + 1])
                kb.act(ctx.p_sb[1][0:wdt, :], ctx.pb[1][0:wdt, :], AF.Gelu,
                       bias=b1sb[0:wdt, 2 * jb + 1:2 * jb + 2])
                kb.tt(ctx.p_sb[2][0:wdt, :], ctx.p_sb[0][0:wdt, :],
                      ctx.p_sb[1][0:wdt, :], ALU.mult)
                kb.dma("scalar", hm_hbm[jb * 128:jb * 128 + wdt, sl],
                       ctx.p_sb[2][0:wdt, :], wk=["HM2"])
        b2col = kb.sbuf(f"{lt}_b2c", [128, FB], F32)
        b26 = din["L2F_b2"].rearrange("l (b p o) -> l b p o", b=FB, o=1)
        for b in range(FB):
            kb.dma("gpsimd", b2col[:, b:b + 1], b26[li, b])
        hmr = hm_hbm.rearrange("(b p) t -> p b t", p=128)
        w2p = hbm_feat(din["L2F_W2p"][li])
        ra_f = ResAdd(kb, ctx, x2_hbm, "X2", x2_hbm, "X2")
        hmall = [ctx.at[4], ctx.at[5], ctx.at[3]]
        w2sl = kb.sbuf(f"{lt}_w2sl", [128, 22, 128], BF16)
        for tt in range(ntt):
            sl = slice(tt * 512, (tt + 1) * 512)
            for s, (b0, b1) in enumerate(((0, 8), (8, 16), (16, 22))):
                kb.dma("sync",
                       hmall[s][:, 0:(b1 - b0) * 512].rearrange("p (b t) -> p b t", t=512),
                       hmr[:, b0:b1, sl], rk=["HM2"])
            for fo in range(FB):
                kb.dma("gpsimd", w2sl[:], w2p[:, :, fo * 128:(fo + 1) * 128])
                pb = ctx.pb[fo % 2]
                for jb in range(22):
                    hsrc = hmall[jb // 8]
                    hap = hsrc[:, (jb % 8) * 512:(jb % 8 + 1) * 512]
                    kb.mm(pb[:, :], w2sl[:, jb, :], hap, jb == 0, jb == 21)
                ev = ctx.evf[fo % 2]
                kb.act(ev[:], pb[:, :], AF.Identity, bias=b2col[:, fo:fo + 1])
                ra_f.add(fo, tt, ev[:])
        ra_f.done()

    x2rr = hbm_feat(x2_hbm)
    for tt in range(ntt):
        sl = slice(tt * 512, (tt + 1) * 512)
        for fo in range(FB):
            ev = ctx.evf[fo % 2]
            kb.op("q_sync", "sync",
                  lambda en, o=ev[:], i=x2rr[:, fo, sl]: en.dma_start(o, i),
                  reads=[x2rr[:, fo, sl], "X2"], writes=[ev[:]], inck=16)
            ra.add(fo, tt, ev[:])
    ra.done()


def emit_layer2(kb, ctx, xin, xinkey, xmid, xout):
    T = kb.T
    nc = kb.nc
    xn = nc.dram_tensor("XN2L", [D, T], BF16).ap()
    ah = nc.dram_tensor("AH2", [D, T], BF16).ap()
    x2 = nc.dram_tensor("X2H", [D, T], F32).ap()
    a2 = nc.dram_tensor("A2H", [D, T], BF16).ap()
    xf2 = nc.dram_tensor("XF2H", [D, T], BF16).ap()
    hm = nc.dram_tensor("HM2H", [2816, T], BF16).ap()
    emit_outer_ln(kb, ctx, xin, xinkey, xn)
    ra = ResAdd(kb, ctx, xin, xinkey, xmid, "XM")
    emit_l2_attn(kb, ctx, xn, ah, ra)
    ra2 = ResAdd(kb, ctx, xmid, "XM", xout, "XO")
    emit_l2_ffn(kb, ctx, xn, x2, a2, xf2, hm, ra2)


EMITTERS[2] = emit_layer2



def emit_l3_attn(kb, ctx, xn_hbm, ra, lt="l3a"):
    T = kb.T
    ntt = T // 512
    nkt = T // 128
    din = ctx.din
    wq = kb.sbuf(f"{lt}_wq", [128, FB, 64], BF16)
    wk = kb.sbuf(f"{lt}_wk", [128, FB, 64], BF16)
    wv = kb.sbuf(f"{lt}_wv", [128, FB, 64], BF16)
    wo = kb.sbuf(f"{lt}_wo", [64, 1024], BF16)
    kb.dma("gpsimd", wq[:], din["L3A_wq"].rearrange("(b p) f -> p b f", p=128))
    kb.dma("gpsimd", wk[:], din["L3A_wk"].rearrange("(b p) f -> p b f", p=128))
    kb.dma("gpsimd", wv[:], din["L3A_wv"].rearrange("(b p) f -> p b f", p=128))
    kb.dma("gpsimd", wo[:], din["L3A_wo"][:])
    bq = bias_col_load(kb, "L3A_bq", din, f"{lt}bq", 64)
    bk = bias_col_load(kb, "L3A_bk", din, f"{lt}bk", 64)
    bo = bias_col_load(kb, "L3A_bo", din, f"{lt}bo", 1024)
    bvrow = kb.sbuf(f"{lt}_bvrow", [1, 64], F32)
    kb.dma("gpsimd", bvrow[:], din["L3A_bv"].rearrange("(o p) -> o p", o=1))

    qt_sb = ctx.at[0][0:64, :]
    kt_sb = ctx.at[1][0:64, :]
    at_sb = ctx.at[2][0:64, :]
    vtok = ctx.vt
    bvbc = kb.sbuf(f"{lt}_bvbc", [128, 64], F32)
    brow = ctx.srow[0:1, :]
    bias_sb = kb.sbuf(f"{lt}_bias", [128, 1], F32)

    kb.memset(vtok[:, :, 64:65], 1.0)
    bcast_row(kb, ctx, ctx.pb[5][:, 0:64], bvrow[:])
    kb.copy(bvbc[:], ctx.pb[5][:, 0:64])

    xr = hbm_feat(xn_hbm)
    for tt in range(ntt):
        st = ctx.stage[tt % 2]
        kb.dma("sync", st[:], xr[:, :, tt * 512:(tt + 1) * 512], rk=["XN"])
        for fi in range(FB):
            kb.mm(ctx.pb[0][0:64, :], wq[:, fi, :], st[:, fi, :], fi == 0, fi == 7)
            kb.mm(ctx.pb[1][0:64, :], wk[:, fi, :], st[:, fi, :], fi == 0, fi == 7)
        kb.act(qt_sb[:, tt * 512:(tt + 1) * 512], ctx.pb[0][0:64, :], AF.Identity,
               bias=bq[:, 0:1])
        kb.act(kt_sb[:, tt * 512:(tt + 1) * 512], ctx.pb[1][0:64, :], AF.Identity,
               bias=bk[:, 0:1])
        for t4 in range(4):
            kt = tt * 4 + t4
            for fi in range(FB):
                kb.mm(ctx.pb[2][:, 0:64], st[:, fi, t4 * 128:(t4 + 1) * 128],
                      wv[:, fi, :], fi == 0, fi == 7)
            kb.tt(vtok[:, kt, 0:64], ctx.pb[2][:, 0:64], bvbc[:], ALU.add)

    for which, src in enumerate((qt_sb, kt_sb)):
        for tt in range(ntt):
            sl = slice(tt * 512, (tt + 1) * 512)
            kb.act(ctx.sqs[:], src[:, sl], AF.Square)
            kb.mm(ctx.pb[3][0:1, :], ctx.ones_b[0:64, 0:1], ctx.sqs[:], True, True)
            kb.red(ctx.bnd[0:1, which, tt:tt + 1], ctx.pb[3][0:1, :], ALU.max)
    kb.red(brow[0:1, 0:1], ctx.bnd[0:1, 0, 0:ntt], ALU.max)
    kb.red(brow[0:1, 1:2], ctx.bnd[0:1, 1, 0:ntt], ALU.max)
    kb.tt(brow[0:1, 2:3], brow[0:1, 0:1], brow[0:1, 1:2], ALU.mult)
    kb.act(brow[0:1, 3:4], brow[0:1, 2:3], AF.Sqrt)
    bcast_row(kb, ctx, ctx.pb[3][:, 0:1], brow[0:1, 3:4])
    kb.ts(bias_sb[:], ctx.pb[3][:, 0:1], -0.125, ALU.mult)

    SCALE = 0.125
    for qt in range(ntt):
        qsl = slice(qt * 512, (qt + 1) * 512)
        nk = 4 * qt + 4
        for g0 in range(0, nk, 6):
            kts = list(range(g0, min(g0 + 6, nk)))
            for j, kt in enumerate(kts):
                kb.mm(ctx.pb[j][:, :], kt_sb[:, kt * 128:(kt + 1) * 128],
                      qt_sb[:, qsl], True, True)
            for j, kt in enumerate(kts):
                kb.act(ctx.p_sb[j][:], ctx.pb[j][:], AF.Exp, bias=bias_sb[:, 0:1],
                       scale=SCALE)
                if kt >= 4 * qt:
                    jrel = kt - 4 * qt
                    kb.tt(ctx.p_sb[j][:], ctx.p_sb[j][:],
                          ctx.am01[:, 512 - 128 * jrel:1024 - 128 * jrel], ALU.mult)
            for j, kt in enumerate(kts):
                kb.mm(ctx.pO[0:65, :], vtok[:, kt, 0:65], ctx.p_sb[j][:],
                      kt == 0, kt == nk - 1)
        kb.recip(ctx.rrow[:], ctx.pO[64:65, :])
        bcast_row(kb, ctx, ctx.pb[0][0:64, :], ctx.rrow[:], nrows=64)
        kb.copy(ctx.rbc[:], ctx.pb[0][0:64, :])
        kb.tt(at_sb[:, qsl], ctx.pO[0:64, :], ctx.rbc[:], ALU.mult)

    for tt in range(ntt):
        sl = slice(tt * 512, (tt + 1) * 512)
        for fo in range(FB):
            kb.mm(ctx.pb[1 + (fo % 2)][:, :], wo[:, fo * 128:(fo + 1) * 128],
                  at_sb[:, sl], True, True)
            ev = ctx.evf[fo % 2]
            kb.act(ev[:], ctx.pb[1 + (fo % 2)][:, :], AF.Identity, bias=bo[:, fo:fo + 1])
            ra.add(fo, tt, ev[:])
    ra.done()


def emit_l3_ffn(kb, ctx, xn_hbm, a_hbm, moe_hbm, wr_hbm, ra, xorig, xorigkey, lt="l3f"):
    T = kb.T
    ntt = T // 512
    nkt = T // 128
    din = ctx.din
    bq = bias_col_load(kb, "L3F_q_b", din, f"{lt}bq", 1024)
    bk = bias_col_load(kb, "L3F_k_b", din, f"{lt}bk", 1024)
    bvrow = kb.sbuf(f"{lt}_bvrow", [1, 1024], F32)
    kb.dma("gpsimd", bvrow[:], din["L3F_v_b"].rearrange("(o p) -> o p", o=1))
    cos_sb = ctx.at[4][:, 0:T]
    sin_sb = ctx.at[5][:, 0:T]
    kb.dma("gpsimd", cos_sb, din["L3F_COS"][:])
    kb.dma("gpsimd", sin_sb, din["L3F_SIN"][:])

    qraw = ctx.at[0]
    kraw = ctx.at[1]
    kh = [ctx.at[2][0:64, :], ctx.at[3][0:64, :]]
    qh = [ctx.at[6][0:64, :], ctx.at[7][0:64, :]]
    vtok = ctx.vt
    rsc = kb.sbuf(f"{lt}_rsc", [64, T], BF16)
    brow = ctx.srow[0:1, :]
    bias_sb = kb.sbuf(f"{lt}_bias", [128, 2], F32)
    bvbc = kb.sbuf(f"{lt}_bvbc", [128, 128], F32)

    kb.memset(vtok[:, :, 64:65], 1.0)
    kb.memset(vtok[:, :, 130:131], 1.0)

    xr = hbm_feat(xn_hbm)
    SC = 0.125

    def rope(raw, base, out):
        kb.tt(out[0:32, 0:T], raw[base:base + 32, 0:T],
              cos_sb[base:base + 32, 0:T], ALU.mult)
        kb.tt(rsc[0:32, 0:T], raw[base + 32:base + 64, 0:T],
              sin_sb[base + 32:base + 64, 0:T], ALU.mult)
        kb.tt(out[0:32, 0:T], out[0:32, 0:T], rsc[0:32, 0:T], ALU.subtract)
        kb.tt(out[32:64, 0:T], raw[base + 32:base + 64, 0:T],
              cos_sb[base + 32:base + 64, 0:T], ALU.mult)
        kb.tt(rsc[32:64, 0:T], raw[base:base + 32, 0:T],
              sin_sb[base:base + 32, 0:T], ALU.mult)
        kb.tt(out[32:64, 0:T], out[32:64, 0:T], rsc[32:64, 0:T], ALU.add)

    wqr = din["L3F_q_wT"].rearrange("(b p) f -> p b f", p=128)
    wkr = din["L3F_k_wT"].rearrange("(b p) f -> p b f", p=128)
    wvr = din["L3F_v_wT"].rearrange("(b p) f -> p b f", p=128)
    for hp in range(8):
        h0 = 2 * hp
        csl = slice(h0 * 64, (h0 + 2) * 64)
        wpq = ctx.wbuf[0]
        wpk = ctx.wbuf[1]
        wpv = kb.sbuf(f"{lt}_wpv", [128, FB, 128], BF16)
        kb.dma("gpsimd", wpq[:, :, 0:128], wqr[:, :, csl])
        kb.dma("gpsimd", wpk[:, :, 0:128], wkr[:, :, csl])
        kb.dma("gpsimd", wpv[:], wvr[:, :, csl])
        bcast_row(kb, ctx, ctx.pb[5][:, 0:128], bvrow[0:1, csl])
        kb.copy(bvbc[:], ctx.pb[5][:, 0:128])
        for tt in range(ntt):
            st = ctx.stage[tt % 2]
            kb.dma("sync", st[:], xr[:, :, tt * 512:(tt + 1) * 512], rk=["XN"])
            tsl = slice(tt * 512, (tt + 1) * 512)
            for fi in range(FB):
                kb.mm(ctx.pb[0][:, :], wpq[:, fi, 0:128], st[:, fi, :], fi == 0, fi == 7)
                kb.mm(ctx.pb[1][:, :], wpk[:, fi, 0:128], st[:, fi, :], fi == 0, fi == 7)
            kb.act(qraw[:, tsl], ctx.pb[0][:, :], AF.Identity, bias=bq[:, hp:hp + 1])
            kb.act(kraw[:, tsl], ctx.pb[1][:, :], AF.Identity, bias=bk[:, hp:hp + 1])
            for t4 in range(4):
                kt = tt * 4 + t4
                for fi in range(FB):
                    kb.mm(ctx.pb[2][:, 0:128], st[:, fi, t4 * 128:(t4 + 1) * 128],
                          wpv[:, fi, :], fi == 0, fi == 7)
                kb.tt(vtok[:, kt, 0:64], ctx.pb[2][:, 0:64], bvbc[:, 0:64], ALU.add)
                kb.tt(vtok[:, kt, 66:130], ctx.pb[2][:, 64:128], bvbc[:, 64:128],
                      ALU.add)
        for hh in range(2):
            rope(qraw, hh * 64, qh[hh])
            rope(kraw, hh * 64, kh[hh])
        for hh in range(2):
            for which, src in enumerate((qh[hh], kh[hh])):
                for tt in range(ntt):
                    sl = slice(tt * 512, (tt + 1) * 512)
                    kb.act(ctx.sqs[:], src[:, sl], AF.Square)
                    kb.mm(ctx.pb[3][0:1, :], ctx.ones_b[0:64, 0:1], ctx.sqs[:],
                          True, True)
                    kb.red(ctx.bnd[0:1, which, tt:tt + 1], ctx.pb[3][0:1, :],
                           ALU.max)
            kb.red(brow[0:1, 0:1], ctx.bnd[0:1, 0, 0:ntt], ALU.max)
            kb.red(brow[0:1, 1:2], ctx.bnd[0:1, 1, 0:ntt], ALU.max)
            kb.tt(brow[0:1, 2:3], brow[0:1, 0:1], brow[0:1, 1:2], ALU.mult)
            kb.act(brow[0:1, 3:4], brow[0:1, 2:3], AF.Sqrt)
            bcast_row(kb, ctx, ctx.pb[3][:, 0:1], brow[0:1, 3:4])
            kb.ts(bias_sb[:, hh:hh + 1], ctx.pb[3][:, 0:1], -SC, ALU.mult)
        for hh in range(2):
            voff = hh * 66
            for qt in range(ntt):
                qsl = slice(qt * 512, (qt + 1) * 512)
                for g0 in range(0, nkt, 6):
                    kts = list(range(g0, min(g0 + 6, nkt)))
                    for j, kt in enumerate(kts):
                        kb.mm(ctx.pb[j][:, :], kh[hh][:, kt * 128:(kt + 1) * 128],
                              qh[hh][:, qsl], True, True)
                    for j, kt in enumerate(kts):
                        kb.act(ctx.p_sb[j][:], ctx.pb[j][:], AF.Exp,
                               bias=bias_sb[:, hh:hh + 1], scale=SC)
                    for j, kt in enumerate(kts):
                        kb.mm(ctx.pO[0:65, :], vtok[:, kt, voff:voff + 65],
                              ctx.p_sb[j][:], kt == 0, kt == nkt - 1)
                kb.recip(ctx.rrow[:], ctx.pO[64:65, :])
                bcast_row(kb, ctx, ctx.pb[0][0:64, :], ctx.rrow[:], nrows=64)
                kb.copy(ctx.rbc[:], ctx.pb[0][0:64, :])
                kb.tt(ctx.p_sb[0][0:64, :], ctx.pO[0:64, :], ctx.rbc[:], ALU.mult)
                h = h0 + hh
                kb.dma("sync", a_hbm[h * 64:(h + 1) * 64, qt * 512:(qt + 1) * 512],
                       ctx.p_sb[0][0:64, :], wk=["AH"])

    gwt = kb.sbuf(f"{lt}_gwt", [128, FB, 8], BF16)
    kb.dma("gpsimd", gwt[:], din["L3F_gate_wT"].rearrange("(b p) f -> p b f", p=128))
    gbrow = kb.sbuf(f"{lt}_gbrow", [1, 8], F32)
    kb.dma("gpsimd", gbrow[:], din["L3F_gate_b"].rearrange("(o p) -> o p", o=1))
    gb_bc = kb.sbuf(f"{lt}_gbbc", [128, 8], F32)
    bcast_row(kb, ctx, ctx.pb[5][:, 0:8], gbrow[:])
    kb.copy(gb_bc[:], ctx.pb[5][:, 0:8])
    wrow_bf = ctx.at[0][0:8, :]
    gl = kb.sbuf(f"{lt}_gl", [128, 8, 8], F32)
    pe8 = kb.sbuf(f"{lt}_pe8", [128, 8], BF16)
    ar = hbm_feat(a_hbm)

    for tt in range(ntt):
        st = ctx.stage[tt % 2]
        kb.dma("sync", st[:], ar[:, :, tt * 512:(tt + 1) * 512], rk=["AH"])
        for t4 in range(4):
            i128 = tt * 4 + t4
            for fi in range(FB):
                kb.mm(ctx.pb[0][:, 0:8], st[:, fi, t4 * 128:(t4 + 1) * 128],
                      gwt[:, fi, :], fi == 0, fi == 7)
            g = gl[:, 0, :]
            kb.tt(g, ctx.pb[0][:, 0:8], gb_bc[:], ALU.add)
            m1 = gl[:, 4, 0:1]
            kb.red(m1, g, ALU.max)
            e1 = gl[:, 5, :]
            kb.ts(e1, g, m1, ALU.subtract)
            kb.act(e1, e1, AF.Exp)
            ssum = gl[:, 4, 1:2]
            kb.red(ssum, e1, ALU.add)
            t2m = gl[:, 6, :]
            kb.ts(t2m, g, m1, ALU.is_ge)
            kb.stt(t2m, t2m, -1e30, g, ALU.mult, ALU.add)
            t2 = gl[:, 4, 2:3]
            kb.red(t2, t2m, ALU.max)
            ind = gl[:, 7, :]
            kb.ts(ind, g, t2, ALU.is_ge)
            kb.tt(e1, e1, ind, ALU.mult)
            rs = gl[:, 4, 3:4]
            kb.recip(rs, ssum)
            kb.ts(pe8[:], e1, rs, ALU.mult)
            kb.tr(ctx.pT[0:8, 0:128], pe8[:], ctx.ident[:])
            kb.copy(wrow_bf[0:8, i128 * 128:(i128 + 1) * 128], ctx.pT[0:8, 0:128])
    kb.dma("sync", wr_hbm[:], wrow_bf[:], wk=["WRH"])

    aw = kb.sbuf(f"{lt}_aw", [128, FB, 512], BF16)
    wbc8 = kb.sbuf(f"{lt}_wbc8", [128, 8, 512], BF16)
    eb = kb.sbuf(f"{lt}_eb", [8, 1024], BF16)
    kb.dma("gpsimd", eb[:], din["L3F_expert_b"][:])
    ewT = din["L3F_expert_wT"]
    mrh = hbm_feat(moe_hbm)
    for tt in range(ntt):
        st = ctx.stage[2]
        kb.dma("scalar", st[:], ar[:, :, tt * 512:(tt + 1) * 512], rk=["AH"])
        src8 = bass.AP(wr_hbm.tensor, tt * 512, [[0, 128], [T, 8], [1, 512]])
        kb.dma("scalar", wbc8[:], src8, rk=["WRH"])
        for half in range(2):
            for e in range(8):
                wa = ctx.wbuf[e % 2]
                er = ewT[e].rearrange("(b p) f -> p b f", p=128)
                kb.dma("gpsimd", wa[:], er[:, :, half * 512:(half + 1) * 512])
                for fb_ in range(FB):
                    kb.tt(aw[:, fb_, :], st[:, fb_, :], wbc8[:, e, :], ALU.mult)
                for fo4 in range(4):
                    fo = half * 4 + fo4
                    for fi in range(FB):
                        kb.mm(ctx.pb[fo4][:, :], wa[:, fi, fo4 * 128:(fo4 + 1) * 128],
                              aw[:, fi, :], (e == 0 and fi == 0), False)
            for fo4 in range(4):
                fo = half * 4 + fo4
                kb.mm(ctx.pb[fo4][:, :], eb[:, fo * 128:(fo + 1) * 128],
                      wrow_bf[0:8, tt * 512:(tt + 1) * 512], False, True)
                kb.copy(ctx.stage[tt % 2][:, fo, :], ctx.pb[fo4][:, :], eng="scalar")
        kb.dma("sync", mrh[:, :, tt * 512:(tt + 1) * 512], ctx.stage[tt % 2][:],
               wk=["MH"])

    fwr = din["L3F_ffn_wT"].rearrange("(b p) f -> p b f", p=128)
    fb_bias = bias_col_load(kb, "L3F_ffn_b", din, f"{lt}fb", 1024)

    xo = hbm_feat(xorig)

    def extra_xn(rdst, fo, tt):
        sl = slice(tt * 512, (tt + 1) * 512)
        tmp = ctx.xrw[2 + (tt % 2)]
        kb.dma("gpsimd", tmp[:], xo[:, fo, sl], rk=[xorigkey])
        kb.tt(tmp[:], tmp[:], ctx.lnb[0][:, sl], ALU.mult)
        kb.tt(tmp[:], tmp[:], ctx.lnb[1][:, sl], ALU.subtract)
        kb.tt(rdst[:], rdst[:], tmp[:], ALU.add)

    ra.extra = extra_xn
    for tt in range(ntt):
        st = ctx.stage[tt % 2]
        kb.dma("sync", st[:], mrh[:, :, tt * 512:(tt + 1) * 512], rk=["MH"])
        sl = slice(tt * 512, (tt + 1) * 512)
        for fh in range(2):
            kb.dma("gpsimd", ctx.wbuf[fh][:], fwr[:, :, fh * 512:(fh + 1) * 512])
            for fo4 in range(4):
                fo = fh * 4 + fo4
                pb = ctx.pb[fo % 2]
                for fi in range(FB):
                    kb.mm(pb[:, :], ctx.wbuf[fh][:, fi, fo4 * 128:(fo4 + 1) * 128],
                          st[:, fi, :], fi == 0, fi == 7)
                ev = ctx.evf[fo % 2]
                kb.act(ev[:], pb[:, :], AF.Identity, bias=fb_bias[:, fo:fo + 1])
                ra.add(fo, tt, ev[:])
    ra.done()


def emit_layer3(kb, ctx, xin, xinkey, xmid, xout):
    T = kb.T
    nc = kb.nc
    xn = nc.dram_tensor("XN3", [D, T], BF16).ap()
    ah = nc.dram_tensor("AH3", [D, T], BF16).ap()
    mh = nc.dram_tensor("MH3", [D, T], BF16).ap()
    wrh = nc.dram_tensor("WR3", [8, T], BF16).ap()
    emit_outer_ln(kb, ctx, xin, xinkey, xn)
    ra = ResAdd(kb, ctx, xin, xinkey, xmid, "XM")
    emit_l3_attn(kb, ctx, xn, ra)
    ra2 = ResAdd(kb, ctx, xmid, "XM", xout, "XO")
    emit_l3_ffn(kb, ctx, xn, ah, mh, wrh, ra2, xin, xinkey)


EMITTERS[3] = emit_layer3


def build_program(layers, T, warrs):
    nc = bass.Bass("TRN2", target_bir_lowering=False)
    keys = keys_for_layers(warrs, layers)
    din = {}
    for k in keys:
        a = warrs[k]
        din[k] = nc.dram_tensor(k, list(a.shape), F32, kind="ExternalInput").ap()
    XIN = nc.dram_tensor("XIN", [D, T], F32, kind="ExternalInput").ap()
    XOUT = nc.dram_tensor("XOUT", [D, T], F32, kind="ExternalOutput").ap()
    XMID = nc.dram_tensor("XMID", [D, T], F32).ap()
    XA = nc.dram_tensor("XA", [D, T], F32).ap()
    kb = KB(nc, T)
    ctx = Ctx(kb, din)
    ctx.load_consts()
    cur, curkey = XIN, "XIN"
    for i, l in enumerate(layers):
        out = XOUT if i == len(layers) - 1 else XA
        kb.begin_layer()
        EMITTERS[l](kb, ctx, cur, curkey, XMID, out)
        kb.end_layer()
        cur, curkey = out, "XO"
    kb.finalize(["XO"])
    return nc, keys


def kernel(**inputs):
    T = 4096
    x = np.asarray(inputs["L0_attn_hidden_states"], dtype=np.float32)
    warrs = prep_weights(inputs, T)
    layer_sets = [[0, 1, 2, 3]]
    cur = [np.ascontiguousarray(x[i].T) for i in range(NCORE)]
    for ls in layer_sets:
        nc, keys = build_program(ls, T, warrs)
        in_maps = []
        for i in range(NCORE):
            m = {k: warrs[k] for k in keys}
            m["XIN"] = cur[i]
            in_maps.append(m)
        res = run_bass_kernel_spmd(nc, in_maps, core_ids=list(range(NCORE)))
        cur = [np.asarray(res.results[i]["XOUT"]) for i in range(NCORE)]
    return np.stack([c.T for c in cur]).astype(np.float32)
